# version 08
# baseline (speedup 1.0000x reference)
import math
import contextlib
import numpy as np

import concourse.bass as bass
import concourse.mybir as mybir
from concourse.bass_utils import run_bass_kernel_spmd

F32 = mybir.dt.float32
BF16 = mybir.dt.bfloat16
AF = mybir.ActivationFunctionType
ALU = mybir.AluOpType

DBG = False

NCORE = 8
N = 16384
D = 1024
TPC = N // NCORE
KD = D // 128
H = 8
DH = 4096
MH = DH // 128
SC = 256
NSC = TPC // SC
FT = 1024
NFT = TPC // FT
LN16 = float(math.log(16.0))

ENGS = ("te", "ve", "sc", "gp", "sy")


class Prog:

    def __init__(self, nc):
        self.nc = nc
        self.q = {e: [] for e in ENGS}
        self.semh = {}
        self.semv = {}
        self.guard = {}

    def sem(self, name, stack):
        self.semh[name] = stack.enter_context(self.nc.semaphore(name))
        self.semv[name] = 0

    def dsem(self, name):
        if name not in self.semh:
            self.semh[name] = self.nc.ctx.enter_context(self.nc.semaphore(name))
            self.semv[name] = 0
        return name

    def do(self, eng, builder, waits=(), inc=None, inc_n=1):
        ws = tuple(w for w in waits if w is not None)
        ev = None
        if inc is not None:
            self.semv[inc] += inc_n
            ev = (inc, self.semv[inc])
        self.q[eng].append((ws, builder, inc, inc_n))
        return ev

    def mark(self, semname):
        return (semname, self.semv[semname])

    def use(self, res):
        return self.guard.get(res)

    def release(self, res, ev):
        self.guard[res] = ev

    def replay(self, eng, obj):
        maxw = {}
        last = None
        last_had_inc = False
        for waits, builder, inc, inc_n in self.q[eng]:
            for s, v in waits:
                if v > maxw.get(s, 0):
                    obj.wait_ge(self.semh[s], v)
                    maxw[s] = v
            if builder is None:
                if inc is not None:
                    assert last is not None and not last_had_inc, \
                        "marker would double-inc"
                    last.then_inc(self.semh[inc], inc_n)
                    last_had_inc = True
                continue
            ins = builder(obj)
            last = ins
            last_had_inc = inc is not None
            if inc is not None:
                ins.then_inc(self.semh[inc], inc_n)


def build_program():
    nc = bass.Bass("TRN2", target_bir_lowering=False)
    dt = nc.dram_tensor

    xT = dt("xT", [D, TPC], F32, kind="ExternalInput").ap()
    xTpre = dt("xTpre", [D, 32], F32, kind="ExternalInput").ap()
    WqT = dt("WqT", [D, D], F32, kind="ExternalInput").ap()
    WkT = dt("WkT", [D, D], F32, kind="ExternalInput").ap()
    WvT = dt("WvT", [D, D], F32, kind="ExternalInput").ap()
    WoT = dt("WoT", [D, D], F32, kind="ExternalInput").ap()
    W1T = dt("W1T", [D, DH], F32, kind="ExternalInput").ap()
    W2T = dt("W2T", [DH, D], F32, kind="ExternalInput").ap()
    fmqT_d = dt("fmqT", [128, 128], F32, kind="ExternalInput").ap()
    fmkT_d = dt("fmkT", [128, 128], F32, kind="ExternalInput").ap()
    fmqb_d = dt("fmqb_dr", [128, 1], F32, kind="ExternalInput").ap()
    fmkb_d = dt("fmkb_dr", [128, 1], F32, kind="ExternalInput").ap()
    fmkbr_d = dt("fmkb_r", [1, 128], F32, kind="ExternalInput").ap()
    rmsw_d = dt("rmsw_dr", [128, 1], F32, kind="ExternalInput").ap()
    b1c_d = dt("b1c_dr", [128, MH], F32, kind="ExternalInput").ap()
    b2c_d = dt("b2c_dr", [128, KD], F32, kind="ExternalInput").ap()
    scl_d = dt("scl_dr", [128, 1], F32, kind="ExternalInput").ap()
    prefm_d = dt("prefm_dr", [128, NCORE], F32, kind="ExternalInput").ap()
    maskA_d = dt("maskA_dr", [128, 512], F32, kind="ExternalInput").ap()
    maskup_d = dt("maskup_dr", [4, 4], F32, kind="ExternalInput").ap()
    eye16r_d = dt("eye16r_dr", [1, 256], F32, kind="ExternalInput").ap()
    eye16m_d = dt("eye16m_dr", [16, 16], F32, kind="ExternalInput").ap()
    ident_d = dt("identd", [128, 128], F32, kind="ExternalInput").ap()
    cpool_d = dt("cpoold", [128, 8], F32, kind="ExternalInput").ap()

    outT = dt("outT", [D, TPC], F32, kind="ExternalOutput").ap()

    dkind = "ExternalOutput" if DBG else "Internal"
    x1sp = dt("x1sp", [D, TPC], F32, kind=dkind).ap()
    x2sp = dt("x2sp", [D, TPC], F32, kind=dkind).ap()
    ccin = dt("ccin", [2176, 128], F32, kind=dkind)
    ccout = dt("ccout", [NCORE * 2176, 128], F32)
    if DBG:
        dbg1 = dt("dbg1", [128, 384], F32, kind="ExternalOutput").ap()
        dbg2 = dt("dbg2", [128, 512], BF16, kind="ExternalOutput").ap()

    top = contextlib.ExitStack()

    def mksb(stack, name, shape, dtyp):
        return stack.enter_context(nc.sbuf_tensor(name, shape, dtyp))

    def mkpt(stack, name, shape):
        return stack.enter_context(nc.psum_tensor(name, list(shape), F32))

    p = Prog(nc)
    for s in ("te", "ve", "ac", "cc", "gp"):
        p.sem(s, top)

    def gdma(dst, src, waits=(), stream="csg"):
        return p.do("gp", lambda g, d=dst, s=src: g.dma_start(out=d, in_=s),
                    waits=waits, inc=p.dsem("D" + stream), inc_n=16)

    def sdma(dst, src, waits=(), stream="css"):
        return p.do("sy", lambda s_, d=dst, s=src: s_.dma_start(out=d, in_=s),
                    waits=waits, inc=p.dsem("D" + stream), inc_n=16)

    def phase_barrier():
        ws = [p.mark("te"), p.mark("ve"), p.mark("ac")]
        ws += [p.mark(s) for s in list(p.semv) if s.startswith("D")]
        for e in ENGS:
            p.do(e, None, waits=ws)

    cpool = mksb(top, "cpool", [128, 8], F32)
    onesb = mksb(top, "onesb", [128, 1], BF16)
    onesr_f = mksb(top, "onesr_f", [1, 128], F32)
    onesr_b = mksb(top, "onesr_b", [1, 128], BF16)
    identb = mksb(top, "identb", [128, 128], BF16)
    identf = mksb(top, "identf", [128, 128], F32)
    maskA = mksb(top, "maskA", [128, 512], F32)
    maskup = mksb(top, "maskup", [4, 4], F32)
    eye16r = mksb(top, "eye16r", [1, 256], BF16)
    eye16m = mksb(top, "eye16m", [16, 16], BF16)
    fmqT_s = mksb(top, "fmqT_s", [128, 128], BF16)
    fmkT_s = mksb(top, "fmkT_s", [128, 128], BF16)
    fmqb_s = mksb(top, "fmqb_s", [128, 1], F32)
    fmkb_s = mksb(top, "fmkb_s", [128, 1], F32)
    fmkbr_s = mksb(top, "fmkbr_s", [1, 128], BF16)
    fmqb_m = mksb(top, "fmqb_m", [128, 1], F32)
    fmqb_mn = mksb(top, "fmqb_mn", [128, 1], F32)
    fmkb_n = mksb(top, "fmkb_n", [128, 1], F32)
    rmsw_s = mksb(top, "rmsw_s", [128, 1], F32)
    b1c = mksb(top, "b1c", [128, MH], F32)
    b2h = mksb(top, "b2h", [128, KD], F32)
    scl = mksb(top, "scl", [128, 1], F32)
    sclh = mksb(top, "sclh", [128, 1], F32)
    prefm = mksb(top, "prefm", [128, NCORE], F32)
    x1pre = mksb(top, "x1pre", [128, KD * 32], F32)
    xn2pre = mksb(top, "xn2pre", [128, KD * 32], BF16)
    mrow = mksb(top, "mrow", [1, 512], F32)
    rrow = mksb(top, "rrow", [1, 512], F32)
    msq = mksb(top, "msq", [1, 512], F32)
    xsqt = mksb(top, "xsqt", [128, 512], BF16)
    lntmp = mksb(top, "lntmp", [128, 512], F32)

    for j, v in enumerate([1e-5, -0.5, 0.5, 1.0 / 1024.0, 1.0 / 128.0,
                           -1.0, 2.0, 1.0]):
        nc.const_aps.register(F32, float(v), cpool[:, j:j + 1])

    psbank = contextlib.ExitStack()
    psA = mkpt(psbank, "psA", (128, 512))
    psB = mkpt(psbank, "psB", (128, 512))
    psC = mkpt(psbank, "psC", (128, 512))
    psD = mkpt(psbank, "psD", (128, 512))
    psE = mkpt(psbank, "psE", (128, 512))
    psF = mkpt(psbank, "psF", (128, 512))
    psR1 = mkpt(psbank, "psR1", (1, 512))
    psR2 = mkpt(psbank, "psR2", (1, 512))

    sdma(cpool[:], cpool_d)
    sdma(maskA[:], maskA_d)
    sdma(maskup[:], maskup_d)
    sdma(fmqb_s[:], fmqb_d)
    sdma(fmkb_s[:], fmkb_d)
    sdma(rmsw_s[:], rmsw_d)
    sdma(b1c[:], b1c_d)
    sdma(b2h[:], b2c_d)
    sdma(scl[:], scl_d)
    sdma(prefm[:], prefm_d)
    gdma(identb[:], ident_d)
    sdma(identf[:], ident_d)
    gdma(eye16r[:], eye16r_d)
    gdma(eye16m[:], eye16m_d)
    gdma(fmqT_s[:], fmqT_d)
    gdma(fmkT_s[:], fmkT_d)
    gdma(fmkbr_s[:], fmkbr_d)
    p.do("gp", lambda g: g.memset(onesb[:], 1.0))
    p.do("gp", lambda g: g.memset(onesr_f[:], 1.0))
    ev_ms = p.do("gp", lambda g: g.memset(onesr_b[:], 1.0), inc="gp")

    b1 = contextlib.ExitStack()
    P16a = mksb(b1, "P16a", [128, KD * 16], BF16)
    xpre_f = mksb(b1, "xpre_f", [128, KD * 32], F32)
    xpre_b = mksb(b1, "xpre_b", [128, KD * 32], BF16)
    xnpre = mksb(b1, "xnpre", [128, KD * 32], BF16)
    sdma(xpre_f[:].rearrange("p (c t) -> p c t", c=KD),
         xTpre.rearrange("(c p) t -> p c t", p=128))
    gdma(xpre_b[:].rearrange("p (c t) -> p c t", c=KD),
         xTpre.rearrange("(c p) t -> p c t", p=128))
    ev_cf = p.mark("Dcss")
    ev_cb = p.mark("Dcsg")
    ev_pf, ev_pb = ev_cf, ev_cb

    p.do("ve", lambda v: v.tensor_scalar(fmqb_m[:], fmqb_s[:], 1.0, -LN16,
                                         ALU.mult, ALU.add), waits=[ev_cf])
    p.do("ve", lambda v: v.tensor_scalar(fmqb_mn[:], fmqb_s[:], -1.0, -LN16,
                                         ALU.mult, ALU.add))
    p.do("ve", lambda v: v.tensor_scalar(fmkb_n[:], fmkb_s[:], -1.0, None, ALU.mult))
    p.do("ve", lambda v: v.tensor_scalar(b2h[:], b2h[:], 0.5, None, ALU.mult))
    ev_dc = p.do("ve", lambda v: v.tensor_scalar(sclh[:], scl[:], 0.5, None,
                                                 ALU.mult), inc="ve")

    def ln_subtile(src_f, src_b, xn_out, ev_f, ev_b, W):
        for k in range(KD):
            p.do("te", lambda t, k=k: t.matmul(
                psR1[0:1, 0:W], onesb[:], src_b(k),
                start=(k == 0), stop=(k == KD - 1)),
                waits=([ev_b, p.use("psR1")] if k == 0 else ()))
        ev_m = p.do("te", None, inc="te")
        for k in range(KD):
            ev_sq = p.do("sc", lambda s, k=k: s.activation(
                xsqt[:, 0:W], src_b(k), AF.Square),
                waits=([ev_b, p.use("xsqt")] if k == 0 else [p.use("xsqt")]),
                inc="ac")
            ev_mm = p.do("te", lambda t, k=k: t.matmul(
                psR2[0:1, 0:W], onesb[:], xsqt[:, 0:W],
                start=(k == 0), stop=(k == KD - 1)),
                waits=([ev_sq, p.use("psR2")] if k == 0 else [ev_sq]),
                inc="te")
            p.release("xsqt", ev_mm)
        ev_q = p.mark("te")
        ev_mr = p.do("ve", lambda v: v.tensor_scalar(
            mrow[0:1, 0:W], psR1[0:1, 0:W], 1.0 / 1024.0, None, ALU.mult),
            waits=[ev_m, p.use("mrow")], inc="ve")
        p.release("psR1", ev_mr)
        ev_msq = p.do("sc", lambda s: s.activation(
            msq[0:1, 0:W], mrow[0:1, 0:W], AF.Square),
            waits=[ev_mr, p.use("msq")], inc="ac")
        ev_var = p.do("ve", lambda v: v.scalar_tensor_tensor(
            rrow[0:1, 0:W], psR2[0:1, 0:W], 1.0 / 1024.0, msq[0:1, 0:W],
            ALU.mult, ALU.subtract),
            waits=[ev_q, ev_msq, p.use("rrow")], inc="ve")
        p.release("psR2", ev_var)
        p.release("msq", ev_var)
        ev_ln = p.do("sc", lambda s: s.activation(rrow[0:1, 0:W], rrow[0:1, 0:W],
                                                  AF.Ln, bias=1e-5),
                     waits=[ev_var], inc="ac")
        ev_r = p.do("sc", lambda s: s.activation(
            rrow[0:1, 0:W], rrow[0:1, 0:W], AF.Exp, scale=-0.5),
            waits=[ev_ln], inc="ac")
        p.do("te", lambda t: t.matmul(psA[:, 0:W], onesr_f[:], mrow[0:1, 0:W],
                                      start=True, stop=True),
             waits=[ev_mr, p.use("psA")])
        ev_bc = p.do("te", lambda t: t.matmul(
            psB[:, 0:W], onesr_f[:], rrow[0:1, 0:W], start=True, stop=True),
            waits=[ev_r, p.use("psB")], inc="te")
        p.release("mrow", ev_bc)
        p.release("rrow", ev_bc)
        ev_done = None
        for k in range(KD):
            ev_sub = p.do("ve", lambda v, k=k: v.tensor_tensor(
                lntmp[:, 0:W], src_f(k), psA[:, 0:W], ALU.subtract),
                waits=([ev_bc, ev_f, p.use("lntmp")] if k == 0 else [ev_done]),
                inc="ve")
            ev_done = p.do("ve", lambda v, k=k: v.tensor_tensor(
                xn_out(k), lntmp[:, 0:W], psB[:, 0:W], ALU.mult),
                waits=[ev_sub], inc="ve")
        p.release("lntmp", ev_done)
        p.release("psA", ev_done)
        p.release("psB", ev_done)
        return ev_done

    def prep_lora(ev_p16, lst, P16, sfx):
        L_tm = mksb(lst, "L_tm" + sfx, [16, 1024], BF16)
        EAZ = mksb(lst, "EAZ" + sfx, [4, 4 * 1024], BF16)
        ET = mksb(lst, "ET" + sfx, [16, 1024], BF16)
        OZ = mksb(lst, "OZ" + sfx, [128, 32 * 16], BF16)
        E2r = mksb(lst, "E2r" + sfx, [1, 4 * 1024], BF16)
        EA = mksb(lst, "EA" + sfx, [4, 1024], BF16)
        trf = mksb(lst, "trf" + sfx, [128, 16], F32)
        ev_c = None
        for k in range(KD):
            ev_f = p.do("sc", lambda s, k=k: s.copy(
                trf[:], P16[:, k * 16:(k + 1) * 16]),
                waits=[ev_p16, p.use("trf")] if k == 0 else [p.use("trf")],
                inc="ac")
            ev_t = p.do("te", lambda t, k=k: t.transpose(
                psC[0:16, 0:128], trf[:], identf[:]),
                waits=[ev_f, p.use("psC")], inc="te")
            p.release("trf", ev_t)
            ev_c = p.do("ve", lambda v, k=k: v.tensor_copy(
                L_tm[:, k * 128:(k + 1) * 128], psC[0:16, 0:128]),
                waits=[ev_t], inc="ve")
            p.release("psC", ev_c)
        ev_ltm = ev_c
        for hf in range(2):
            ev = p.do("te", lambda t, hf=hf: t.matmul(
                psC[0:4, 0:512], eye16m[:, 0:8:2],
                L_tm[:, hf * 512:(hf + 1) * 512], start=True, stop=True),
                waits=[ev_ltm, p.use("psC")], inc="te")
            ev_c = p.do("ve", lambda v, hf=hf: v.tensor_copy(
                EA[:, hf * 512:(hf + 1) * 512], psC[0:4, 0:512]),
                waits=[ev], inc="ve")
            p.release("psC", ev_c)
        ev_ea = ev_c
        for i in range(4):
            for hf in range(2):
                ev = p.do("te", lambda t, i=i, hf=hf: t.matmul(
                    psC[0:1, 0:512], eye16m[:, 8 + 2 * i:9 + 2 * i],
                    L_tm[:, hf * 512:(hf + 1) * 512], start=True, stop=True),
                    waits=[p.use("psC")], inc="te")
                ev_c = p.do("ve", lambda v, i=i, hf=hf: v.tensor_copy(
                    E2r[0:1, i * 1024 + hf * 512: i * 1024 + (hf + 1) * 512],
                    psC[0:1, 0:512]), waits=[ev], inc="ve")
                p.release("psC", ev_c)
        ev_e2 = ev_c
        ev_eaz = None
        for i in range(4):
            ev_eaz = p.do("ve", lambda v, i=i: v.tensor_scalar(
                EAZ[:, i * 1024:(i + 1) * 1024], EA[:], maskup[:, i:i + 1],
                None, ALU.mult), waits=[ev_ea], inc="ve")
        for hf in range(2):
            ilist = (0, 1) if hf == 0 else (2, 3)
            items = [(i, a2) for i in ilist for a2 in range(4)]
            for idx, (i, a2) in enumerate(items):
                r = a2 * 4 + i
                c0 = (i - ilist[0]) * 256
                p.do("te", lambda t, r=r, i=i, a2=a2, c0=c0, idx=idx: t.matmul(
                    psC[0:16, c0:c0 + 256], eye16r[0:1, r * 16:(r + 1) * 16],
                    E2r[0:1, i * 1024 + a2: i * 1024 + a2 + 1021:4],
                    start=(a2 == 0), stop=(a2 == 3)),
                    waits=([ev_e2, p.use("psC")] if idx == 0 else ()))
            ev = p.do("te", None, inc="te")
            ev_c = p.do("ve", lambda v, hf=hf: v.tensor_copy(
                ET[:, hf * 512:(hf + 1) * 512], psC[0:16, 0:512]),
                waits=[ev], inc="ve")
            p.release("psC", ev_c)
        ev_z = p.do("ve", lambda v: v.memset(OZ[:], 0.0), inc="ve")
        ev_oz = None
        for k in range(32):
            kc, a2 = k % 8, k // 8
            ev_oz = p.do("ve", lambda v, k=k, kc=kc, a2=a2: v.tensor_copy(
                OZ[:, k * 16 + a2 * 4: k * 16 + a2 * 4 + 4],
                P16[:, kc * 16 + 9: kc * 16 + 16:2]),
                waits=([ev_z, ev_p16] if k == 0 else ()), inc="ve")
        ev_lora = ev_oz
        return ev_lora, dict(L_tm=L_tm, EAZ=EAZ, ET=ET, OZ=OZ, P16=P16)

    def ffn_phase(fst, src, dst, ev_lora, lora, with_prep, ev_xnpre, xnpre_t,
                  pre_resid, sfx=""):
        P16, EAZ, ET, OZ = lora["P16"], lora["EAZ"], lora["ET"], lora["OZ"]
        xf_t = mksb(fst, "xf_t" + sfx, [128, KD * FT], F32)
        dtmp = mksb(fst, "dtmp" + sfx, [128, 2 * 512], F32)
        ptmp = mksb(fst, "ptmp" + sfx, [128, 32], F32)
        xb_t = mksb(fst, "xb_t" + sfx, [128, KD * 512], BF16)
        xn_t = mksb(fst, "xn_t" + sfx, [128, KD * FT], BF16)
        h_t = mksb(fst, "h_t" + sfx, [128, MH * FT], BF16)
        Us = mksb(fst, "Us" + sfx, [4, FT], BF16)
        Wsc = mksb(fst, "Wsc" + sfx, [16, FT], BF16)
        w1m = [mksb(fst, "w1m0" + sfx, [128, KD * 128], BF16),
               mksb(fst, "w1m1" + sfx, [128, KD * 128], BF16)]
        w2m = [mksb(fst, "w2m0" + sfx, [128, MH * 128], BF16),
               mksb(fst, "w2m1" + sfx, [128, MH * 128], BF16)]
        if with_prep:
            hpre = mksb(fst, "hpre" + sfx, [128, MH * 32], BF16)
            Uspre = mksb(fst, "Uspre" + sfx, [4, 32], BF16)
            Wscpre = mksb(fst, "Wscpre" + sfx, [16, 32], BF16)

        srcv = src.rearrange("(c p) n -> p c n", p=128)

        def load_w1(m, par):
            return gdma(w1m[par][:].rearrange("p (k c) -> p k c", k=KD),
                        W1T[:, m * 128:(m + 1) * 128]
                        .rearrange("(k p) c -> p k c", p=128),
                        waits=[p.use("w1m%d" % par + sfx)],
                        stream="w1m%d" % par + sfx)

        def load_w2(mo, par):
            return gdma(w2m[par][:].rearrange("p (k c) -> p k c", k=MH),
                        W2T[:, mo * 128:(mo + 1) * 128]
                        .rearrange("(k p) c -> p k c", p=128),
                        waits=[p.use("w2m%d" % par + sfx)],
                        stream="w2m%d" % par + sfx)

        if with_prep:
            for k in range(KD):
                p.do("te", lambda t, k=k: t.matmul(
                    psC[0:4, 0:32], P16[:, k * 16 + 1:k * 16 + 8:2],
                    xnpre_t[:, k * 32:(k + 1) * 32],
                    start=(k == 0), stop=(k == KD - 1)),
                    waits=([ev_lora, ev_xnpre, p.use("psC")] if k == 0 else ()))
            ev = p.do("te", None, inc="te")
            ev_c = p.do("ve", lambda v: v.tensor_scalar(
                Uspre[:], psC[0:4, 0:32], scl[0:4, 0:1], None, ALU.mult),
                waits=[ev], inc="ve")
            p.release("psC", ev_c)

        ev_last_store = None
        for t in range(NFT):
            if t > 0:
                p.do("ve", None, waits=[p.mark("te")])
                p.do("sc", None, waits=[p.mark("te")])
            ev_xf = sdma(xf_t[:].rearrange("p (c w) -> p c w", c=KD),
                         srcv[:, :, t * FT:(t + 1) * FT],
                         waits=[p.use("xf_t" + sfx)], stream="xf" + sfx)
            for w in range(FT // 512):
                ev_xb = gdma(xb_t[:].rearrange("p (c w) -> p c w", c=KD),
                             srcv[:, :, t * FT + w * 512: t * FT + (w + 1) * 512],
                             waits=[p.use("xb_t" + sfx)], stream="xb" + sfx)
                ev_xn = ln_subtile(
                    lambda k, t=t, w=w: xf_t[:, k * FT + w * 512:k * FT + (w + 1) * 512],
                    lambda k: xb_t[:, k * 512:(k + 1) * 512],
                    lambda k, t=t, w=w: xn_t[:, k * FT + w * 512:k * FT + (w + 1) * 512],
                    ev_xf, ev_xb, 512)
                p.release("xb_t" + sfx, p.mark("te"))
            ev_xn_all = p.mark("ve")
            for w in range(FT // 512):
                for k in range(KD):
                    p.do("te", lambda t_, k=k, w=w, t=t: t_.matmul(
                        psC[0:4, 0:512], P16[:, k * 16 + 1:k * 16 + 8:2],
                        xn_t[:, k * FT + w * 512: k * FT + (w + 1) * 512],
                        start=(k == 0), stop=(k == KD - 1)),
                        waits=([ev_xn_all, ev_lora, p.use("psC")] if k == 0 else ()))
                ev = p.do("te", None, inc="te")
                ev_c = p.do("ve", lambda v, w=w: v.tensor_scalar(
                    Us[:, w * 512:(w + 1) * 512], psC[0:4, 0:512],
                    scl[0:4, 0:1], None, ALU.mult), waits=[ev], inc="ve")
                p.release("psC", ev_c)
            ev_us = p.mark("ve")
            ev_w1 = load_w1(0, 0)
            evw = {0: ev_w1}
            for m in range(MH):
                par = m % 2
                if m + 1 < MH:
                    evw[m + 1] = load_w1(m + 1, (m + 1) % 2)
                for w in range(FT // 512):
                    bank, bn = (psA, "psA") if w == 0 else (psB, "psB")
                    for k in range(KD):
                        p.do("te", lambda t_, m=m, k=k, w=w, par=par, bank=bank, t=t: t_.matmul(
                            bank[:, 0:512], w1m[par][:, k * 128:(k + 1) * 128],
                            xn_t[:, k * FT + w * 512: k * FT + (w + 1) * 512],
                            start=(k == 0), stop=False),
                            waits=([evw[m], ev_us, p.use(bn)] if k == 0 else ()))
                    p.do("te", lambda t_, m=m, w=w, bank=bank: t_.matmul(
                        bank[:, 0:512],
                        EAZ[0:4, (m // 8) * 1024 + (m % 8) * 128:
                            (m // 8) * 1024 + (m % 8) * 128 + 128],
                        Us[0:4, w * 512:(w + 1) * 512], start=False, stop=True))
                    ev_mm = p.do("te", None, inc="te")
                    ev_g = p.do("sc", lambda s, m=m, w=w, bank=bank, t=t: s.activation(
                        h_t[:, m * FT + w * 512: m * FT + (w + 1) * 512],
                        bank[:, 0:512], AF.Gelu, bias=b1c[:, m:m + 1]),
                        waits=[ev_mm], inc="ac")
                    p.release(bn, ev_g)
                p.release("w1m%d" % par + sfx, p.mark("te"))
                if with_prep and t == 0:
                    for k in range(KD):
                        p.do("te", lambda t_, m=m, k=k, par=par: t_.matmul(
                            psC[:, 0:32], w1m[par][:, k * 128:(k + 1) * 128],
                            xnpre_t[:, k * 32:(k + 1) * 32],
                            start=(k == 0), stop=False),
                            waits=[p.use("psC")] if k == 0 else ())
                    p.do("te", lambda t_, m=m: t_.matmul(
                        psC[:, 0:32],
                        EAZ[0:4, (m // 8) * 1024 + (m % 8) * 128:
                            (m // 8) * 1024 + (m % 8) * 128 + 128],
                        Uspre[0:4, :], start=False, stop=True))
                    ev_mm = p.do("te", None, inc="te")
                    ev_g = p.do("sc", lambda s, m=m: s.activation(
                        hpre[:, m * 32:(m + 1) * 32], psC[:, 0:32],
                        AF.Gelu, bias=b1c[:, m:m + 1]), waits=[ev_mm], inc="ac")
                    p.release("psC", ev_g)
                    p.release("w1m%d" % par + sfx, p.mark("te"))
            ev_h = p.mark("ac")
            for w in range(FT // 512):
                for k in range(MH):
                    p.do("te", lambda t_, k=k, w=w, t=t: t_.matmul(
                        psC[0:16, 0:512], OZ[:, k * 16:(k + 1) * 16],
                        h_t[:, k * FT + w * 512: k * FT + (w + 1) * 512],
                        start=(k == 0), stop=(k == MH - 1)),
                        waits=([ev_h, p.use("psC")] if k == 0 else ()))
                ev = p.do("te", None, inc="te")
                ev_c = p.do("ve", lambda v, w=w: v.tensor_scalar(
                    Wsc[:, w * 512:(w + 1) * 512], psC[0:16, 0:512],
                    scl[0:16, 0:1], None, ALU.mult), waits=[ev], inc="ve")
                p.release("psC", ev_c)
            ev_wsc = p.mark("ve")
            if with_prep and t == 0:
                for k in range(MH):
                    p.do("te", lambda t_, k=k: t_.matmul(
                        psC[0:16, 0:32], OZ[:, k * 16:(k + 1) * 16],
                        hpre[:, k * 32:(k + 1) * 32],
                        start=(k == 0), stop=(k == MH - 1)),
                        waits=[p.use("psC")] if k == 0 else ())
                ev = p.do("te", None, inc="te")
                ev_c = p.do("ve", lambda v: v.tensor_scalar(
                    Wscpre[:], psC[0:16, 0:32], scl[0:16, 0:1], None, ALU.mult),
                    waits=[ev], inc="ve")
                p.release("psC", ev_c)
            ev_w2 = load_w2(0, 0)
            evw2 = {0: ev_w2}
            for mo in range(KD):
                par = mo % 2
                if mo + 1 < KD:
                    evw2[mo + 1] = load_w2(mo + 1, (mo + 1) % 2)
                for w in range(FT // 512):
                    bank, bn = (psA, "psA") if w == 0 else (psB, "psB")
                    for k in range(MH):
                        p.do("te", lambda t_, mo=mo, k=k, w=w, par=par, bank=bank, t=t: t_.matmul(
                            bank[:, 0:512], w2m[par][:, k * 128:(k + 1) * 128],
                            h_t[:, k * FT + w * 512: k * FT + (w + 1) * 512],
                            start=(k == 0), stop=False),
                            waits=([evw2[mo], ev_wsc, p.use(bn)] if k == 0 else ()))
                    p.do("te", lambda t_, mo=mo, w=w, bank=bank: t_.matmul(
                        bank[:, 0:512], ET[:, mo * 128:(mo + 1) * 128],
                        Wsc[0:16, w * 512:(w + 1) * 512], start=False, stop=True))
                    ev_mm = p.do("te", None, inc="te")
                    ev_h2 = p.do("sc", lambda s, mo=mo, w=w, bank=bank: s.activation(
                        dtmp[:, w * 512:(w + 1) * 512], bank[:, 0:512],
                        AF.Identity, bias=b2h[:, mo:mo + 1], scale=0.5),
                        waits=[ev_mm, p.use("dt%d" % w + sfx)], inc="ac")
                    ev_e = p.do("ve", lambda v, mo=mo, w=w, t=t: v.tensor_tensor(
                        xf_t[:, mo * FT + w * 512: mo * FT + (w + 1) * 512],
                        dtmp[:, w * 512:(w + 1) * 512],
                        xf_t[:, mo * FT + w * 512: mo * FT + (w + 1) * 512],
                        ALU.add), waits=[ev_h2], inc="ve")
                    p.release("dt%d" % w + sfx, ev_e)
                    p.release(bn, ev_h2)
                ev_st = sdma(dst[mo * 128:(mo + 1) * 128, t * FT:(t + 1) * FT],
                             xf_t[:, mo * FT:(mo + 1) * FT],
                             waits=[p.mark("ve")], stream="st")
                ev_last_store = ev_st
                if with_prep and t == NFT - 1:
                    for k in range(MH):
                        p.do("te", lambda t_, mo=mo, k=k, par=par: t_.matmul(
                            psC[:, 0:32], w2m[par][:, k * 128:(k + 1) * 128],
                            hpre[:, k * 32:(k + 1) * 32],
                            start=(k == 0), stop=False),
                            waits=[p.use("psC")] if k == 0 else ())
                    p.do("te", lambda t_, mo=mo: t_.matmul(
                        psC[:, 0:32], ET[:, mo * 128:(mo + 1) * 128],
                        Wscpre[0:16, :], start=False, stop=True))
                    ev_mm = p.do("te", None, inc="te")
                    ev_h2 = p.do("sc", lambda s, mo=mo: s.activation(
                        ptmp[:], psC[:, 0:32], AF.Identity,
                        bias=b2h[:, mo:mo + 1], scale=0.5),
                        waits=[ev_mm, p.use("pt" + sfx)], inc="ac")
                    ev_e = p.do("ve", lambda v, mo=mo: v.tensor_tensor(
                        x1pre[:, mo * 32:(mo + 1) * 32], ptmp[:],
                        pre_resid[:, mo * 32:(mo + 1) * 32], ALU.add),
                        waits=[ev_h2], inc="ve")
                    p.release("pt" + sfx, ev_e)
                    p.release("psC", ev_h2)
                p.release("w2m%d" % par + sfx, p.mark("te"))
            p.release("xf_t" + sfx, ev_last_store)
        return ev_last_store, p.mark("ve")

    p.do("te", None, waits=[ev_cf, ev_cb, ev_ms, ev_dc])
    ev_xnpre = ln_subtile(lambda k: xpre_f[:, k * 32:(k + 1) * 32],
                          lambda k: xpre_b[:, k * 32:(k + 1) * 32],
                          lambda k: xnpre[:, k * 32:(k + 1) * 32],
                          ev_pf, ev_pb, 32)
    ev_p16 = None
    for k in range(KD):
        ev_p16 = p.do("ve", lambda v, k=k: v.tensor_copy(
            P16a[:, k * 16:(k + 1) * 16], xnpre[:, k * 32 + 1:k * 32 + 17]),
            waits=[ev_xnpre] if k == 0 else (), inc="ve")
    ev_loraA, loraA = prep_lora(ev_p16, b1, P16a, "a")
    ev_ffn1_store, ev_x1pre = ffn_phase(b1, xT, x1sp, ev_loraA, loraA, True,
                                        ev_xnpre, xnpre, xpre_f)
    phase_barrier()
    b1.close()

    attn = contextlib.ExitStack()
    xn2 = mksb(attn, "xn2", [128, KD * TPC], BF16)
    ykall = mksb(attn, "ykall", [128, H * TPC], BF16)
    Scum = mksb(attn, "Scum", [128, H * 7 * 256], BF16)
    vall = mksb(attn, "vall", [128, 16 * 1024], BF16)
    b2aux = contextlib.ExitStack()
    opre = mksb(b2aux, "opre", [128, H * 32], BF16)
    vpre = mksb(b2aux, "vpre", [32, H * 128], BF16)
    x2pre = mksb(b2aux, "x2pre", [128, KD * 16], F32)
    Sstage = mksb(b2aux, "Sstage", [128, 256], F32)

    b2 = contextlib.ExitStack()
    x1f_t = mksb(b2, "x1f_t", [128, KD * 512], F32)
    x1b_t = mksb(b2, "x1b_t", [128, KD * 512], BF16)
    x1v = x1sp.rearrange("(c p) n -> p c n", p=128)
    for w in range(TPC // 512):
        ev_f = sdma(x1f_t[:].rearrange("p (c w) -> p c w", c=KD),
                    x1v[:, :, w * 512:(w + 1) * 512],
                    waits=[ev_ffn1_store, p.use("x1f_t")], stream="x1f")
        ev_b = gdma(x1b_t[:].rearrange("p (c w) -> p c w", c=KD),
                    x1v[:, :, w * 512:(w + 1) * 512],
                    waits=[ev_ffn1_store, p.use("x1b_t")], stream="x1b")
        ev_xn2 = ln_subtile(
            lambda k, w=w: x1f_t[:, k * 512:(k + 1) * 512],
            lambda k, w=w: x1b_t[:, k * 512:(k + 1) * 512],
            lambda k, w=w: xn2[:, k * TPC + w * 512: k * TPC + (w + 1) * 512],
            ev_f, ev_b, 512)
        p.release("x1f_t", p.mark("ve"))
        p.release("x1b_t", p.mark("te"))
    ev_xn2 = p.mark("ve")
    x1pre_b = mksb(b2, "x1pre_b", [128, KD * 32], BF16)
    ev_x1pb = p.do("sc", lambda s: s.activation(x1pre_b[:], x1pre[:], AF.Copy),
                   waits=[ev_x1pre], inc="ac")
    ev_xn2pre = ln_subtile(lambda k: x1pre[:, k * 32:(k + 1) * 32],
                           lambda k: x1pre_b[:, k * 32:(k + 1) * 32],
                           lambda k: xn2pre[:, k * 32:(k + 1) * 32],
                           ev_x1pre, ev_x1pb, 32)
    if DBG:
        sdma(dbg1[:, 0:256], x1pre[:], waits=[ev_x1pre], stream="dbg")
        sdma(dbg2[:, 0:256], xn2pre[:], waits=[ev_xn2pre], stream="dbg")
    wv = mksb(b2, "wv", [128, KD * 1024], BF16)
    ev_wv = gdma(wv[:].rearrange("p (k c) -> p k c", k=KD),
                 WvT.rearrange("(k p) c -> p k c", p=128), stream="wv")
    for tc in range(16):
        for half in range(2):
            bank, bn = (psA, "psA") if half == 0 else (psB, "psB")
            for k in range(KD):
                p.do("te", lambda t, tc=tc, k=k, half=half, bank=bank: t.matmul(
                    bank[:, 0:512], xn2[:, k * TPC + tc * 128:k * TPC + (tc + 1) * 128],
                    wv[:, k * 1024 + half * 512: k * 1024 + (half + 1) * 512],
                    start=(k == 0), stop=(k == KD - 1)),
                    waits=([ev_wv, ev_xn2, p.use(bn)] if k == 0 else ()))
            ev = p.do("te", None, inc="te")
            ev_c = p.do("ve", lambda v, tc=tc, half=half, bank=bank: v.tensor_copy(
                vall[:, tc * 1024 + half * 512: tc * 1024 + (half + 1) * 512],
                bank[:, 0:512]), waits=[ev], inc="ve")
            p.release(bn, ev_c)
    ev_vall = p.mark("ve")
    for h in range(H):
        for k in range(KD):
            p.do("te", lambda t, h=h, k=k: t.matmul(
                psC[0:32, 0:128], xn2pre[:, k * 32:(k + 1) * 32],
                wv[:, k * 1024 + h * 128: k * 1024 + (h + 1) * 128],
                start=(k == 0), stop=(k == KD - 1)),
                waits=([ev_wv, ev_xn2pre, p.use("psC")] if k == 0 else ()))
        ev = p.do("te", None, inc="te")
        ev_c = p.do("ve", lambda v, h=h: v.tensor_copy(
            vpre[:, h * 128:(h + 1) * 128], psC[0:32, 0:128]),
            waits=[ev], inc="ve")
        p.release("psC", ev_c)
    ev_vpre = p.mark("ve")
    phase_barrier()
    b2.close()

    b2h_st = contextlib.ExitStack()
    wkh = [mksb(b2h_st, "wkh0", [128, KD * 128], BF16),
           mksb(b2h_st, "wkh1", [128, KD * 128], BF16)]
    wqh = [mksb(b2h_st, "wqh0", [128, KD * 128], BF16),
           mksb(b2h_st, "wqh1", [128, KD * 128], BF16)]
    kT = mksb(b2h_st, "kT", [128, TPC], BF16)
    ktok = mksb(b2h_st, "ktok", [128, 16 * 256], BF16)
    qpreT = mksb(b2h_st, "qpreT", [128, 32], BF16)
    kpreT = mksb(b2h_st, "kpreT", [128, 32], BF16)
    qpf = mksb(b2h_st, "qpf", [128, 2 * 32], BF16)
    kpf = mksb(b2h_st, "kpf", [128, 2 * 32], BF16)
    Apre = mksb(b2h_st, "Apre", [32, 32], BF16)
    Scur = mksb(b2h_st, "Scur", [128, 256], F32)
    osqp = mksb(b2h_st, "osqp", [128, 32], BF16)
    rbp = mksb(b2h_st, "rbp", [128, 32], F32)

    def head_w(dst, par, name, W, h):
        return gdma(dst[par][:].rearrange("p (k c) -> p k c", k=KD),
                    W[:, h * 128:(h + 1) * 128].rearrange("(k p) c -> p k c", p=128),
                    waits=[p.use(name + str(par))], stream=name + str(par))

    evk = {0: head_w(wkh, 0, "wkh", WkT, 0)}
    evq = {0: head_w(wqh, 0, "wqh", WqT, 0)}
    for h in range(H):
        par = h % 2
        if h + 1 < H:
            evk[h + 1] = head_w(wkh, (h + 1) % 2, "wkh", WkT, h + 1)
            evq[h + 1] = head_w(wqh, (h + 1) % 2, "wqh", WqT, h + 1)
        for w in range(4):
            bank, bn = (psA, "psA") if w % 2 == 0 else (psB, "psB")
            for k in range(KD):
                p.do("te", lambda t, h=h, k=k, w=w, par=par, bank=bank: t.matmul(
                    bank[:, 0:512], wkh[par][:, k * 128:(k + 1) * 128],
                    xn2[:, k * TPC + w * 512: k * TPC + (w + 1) * 512],
                    start=(k == 0), stop=(k == KD - 1)),
                    waits=([evk[h], ev_xn2, p.use(bn)] if k == 0 else ()))
            ev = p.do("te", None, inc="te")
            ev_c = p.do("ve", lambda v, w=w, bank=bank: v.tensor_copy(
                kT[:, w * 512:(w + 1) * 512], bank[:, 0:512]),
                waits=[ev, p.use("kT") if w == 0 else None], inc="ve")
            p.release(bn, ev_c)
        ev_kT = p.mark("ve")
        for w in range(4):
            bank, bn = (psC, "psC") if w % 2 == 0 else (psD, "psD")
            ev = p.do("te", lambda t, w=w, bank=bank: t.matmul(
                bank[:, 0:512], fmkT_s[:], kT[:, w * 512:(w + 1) * 512],
                start=True, stop=True),
                waits=[ev_kT, p.use(bn)], inc="te")
            ev_c = p.do("ve", lambda v, h=h, w=w, bank=bank: v.tensor_copy(
                ykall[:, h * TPC + w * 512: h * TPC + (w + 1) * 512],
                bank[:, 0:512]), waits=[ev], inc="ve")
            p.release(bn, ev_c)
        for g in range(4):
            bank, bn = (psE, "psE") if g % 2 == 0 else (psF, "psF")
            for j in range(4):
                tc = g * 4 + j
                p.do("te", lambda t, tc=tc, j=j, bank=bank: t.matmul(
                    bank[:, j * 128:(j + 1) * 128],
                    kT[:, tc * 128:(tc + 1) * 128], fmkT_s[:],
                    start=True, stop=False),
                    waits=([ev_kT, p.use(bn)] if j == 0 else ()))
                p.do("te", lambda t, tc=tc, j=j, bank=bank: t.matmul(
                    bank[:, j * 128:(j + 1) * 128],
                    onesr_b[:], fmkbr_s[:], start=False, stop=True))
            ev = p.do("te", None, inc="te")
            p.do("sc", lambda s, g=g, bank=bank: s.activation(
                ktok[:, g * 512:(g + 1) * 512], bank[:, 0:512], AF.Exp),
                waits=[ev, p.use("ktok") if g == 0 else None])
            ev_x = p.do("sc", lambda s, g=g, bank=bank: s.activation(
                ktok[:, 2048 + g * 512: 2048 + (g + 1) * 512],
                bank[:, 0:512], AF.Exp, scale=-1.0), inc="ac")
            p.release(bn, ev_x)
        ev_ktok = p.mark("ac")
        p.release("kT", p.mark("te"))
        for sc_i in range(NSC):
            for tc2 in range(2):
                tc = sc_i * 2 + tc2
                for fc in range(2):
                    bank = psA if fc == 0 else psB
                    p.do("te", lambda t, tc=tc, fc=fc, h=h, bank=bank,
                         tc2=tc2: t.matmul(
                        bank[:, 0:128],
                        ktok[:, fc * 2048 + tc * 128: fc * 2048 + (tc + 1) * 128],
                        vall[:, tc * 1024 + h * 128: tc * 1024 + (h + 1) * 128],
                        start=(tc2 == 0), stop=(tc2 == 1)),
                        waits=([ev_ktok, ev_vall, p.use("psA"), p.use("psB")]
                               if (tc2 == 0 and fc == 0) else ()))
            ev_s = p.do("te", None, inc="te")
            ev_add = None
            for fc in range(2):
                bank = psA if fc == 0 else psB
                if sc_i == 0:
                    ev_add = p.do("ve", lambda v, fc=fc, bank=bank: v.tensor_copy(
                        Scur[:, fc * 128:(fc + 1) * 128], bank[:, 0:128]),
                        waits=([ev_s, p.use("Scur")] if fc == 0 else ()),
                        inc="ve")
                else:
                    ev_add = p.do("ve", lambda v, fc=fc, bank=bank: v.tensor_tensor(
                        Scur[:, fc * 128:(fc + 1) * 128], bank[:, 0:128],
                        Scur[:, fc * 128:(fc + 1) * 128], ALU.add),
                        waits=([ev_s, p.use("Scur")] if fc == 0 else ()),
                        inc="ve")
            p.release("psA", ev_add)
            p.release("psB", ev_add)
            p.release("Scur", ev_add)
            if sc_i < NSC - 1:
                ev_cp = p.do("sc", lambda s, h=h, sc_i=sc_i: s.copy(
                    Scum[:, (h * 7 + sc_i) * 256:(h * 7 + sc_i + 1) * 256],
                    Scur[:]), waits=[ev_add], inc="ac")
                p.release("Scur", ev_cp)
            else:
                ev_st = p.do("sc", lambda s: s.copy(Sstage[:], Scur[:]),
                             waits=[ev_add, p.use("Sstage")], inc="ac")
                p.release("Scur", ev_st)
                ev_dma = sdma(ccin.ap()[h * 256:(h + 1) * 256, :]
                              .rearrange("(fc p) d -> p fc d", p=128),
                              Sstage[:].rearrange("p (fc d) -> p fc d", fc=2),
                              waits=[ev_st], stream="ci")
                p.release("Sstage", ev_dma)
        p.release("ktok", p.mark("te"))
        for dstT, wt, evw, nm in ((qpreT, wqh, evq, "q"), (kpreT, wkh, evk, "k")):
            for k in range(KD):
                p.do("te", lambda t, dstT=dstT, wt=wt, k=k, par=par, h=h, evw=evw: t.matmul(
                    psC[:, 0:32], wt[par][:, k * 128:(k + 1) * 128],
                    xn2pre[:, k * 32:(k + 1) * 32],
                    start=(k == 0), stop=(k == KD - 1)),
                    waits=([evw[h], ev_xn2pre, p.use("psC")] if k == 0 else ()))
            ev = p.do("te", None, inc="te")
            ev_c = p.do("ve", lambda v, dstT=dstT: v.tensor_copy(dstT[:], psC[:, 0:32]),
                        waits=[ev], inc="ve")
            p.release("psC", ev_c)
        ev_pre_kq = p.mark("ve")
        p.release("wqh%d" % par, p.mark("te"))
        p.release("wkh%d" % par, p.mark("te"))
        for srcT, dstF, fmw, bpos, bneg in (
                (qpreT, qpf, fmqT_s, fmqb_m, fmqb_mn),
                (kpreT, kpf, fmkT_s, fmkb_s, fmkb_n)):
            ev = p.do("te", lambda t, srcT=srcT, fmw=fmw: t.matmul(
                psC[:, 0:32], fmw[:], srcT[:], start=True, stop=True),
                waits=[ev_pre_kq, p.use("psC")], inc="te")
            p.do("sc", lambda s, dstF=dstF, bpos=bpos: s.activation(
                dstF[:, 0:32], psC[:, 0:32], AF.Exp, bias=bpos[:]), waits=[ev])
            ev_x = p.do("sc", lambda s, dstF=dstF, bneg=bneg: s.activation(
                dstF[:, 32:64], psC[:, 0:32], AF.Exp, bias=bneg[:], scale=-1.0),
                inc="ac")
            p.release("psC", ev_x)
        ev_pf_ = p.mark("ac")
        for fc in range(2):
            p.do("te", lambda t, fc=fc: t.matmul(
                psC[0:32, 0:32], kpf[:, fc * 32:(fc + 1) * 32],
                qpf[:, fc * 32:(fc + 1) * 32],
                start=(fc == 0), stop=(fc == 1)),
                waits=([ev_pf_, p.use("psC")] if fc == 0 else ()))
        ev = p.do("te", None, inc="te")
        ev_am = p.do("ve", lambda v: v.tensor_tensor(
            Apre[:], psC[0:32, 0:32], maskA[0:32, 0:32], ALU.mult),
            waits=[ev, p.use("Apre")], inc="ve")
        p.release("psC", ev_am)
        ev_o = p.do("te", lambda t, h=h: t.matmul(
            psD[:, 0:32], vpre[:, h * 128:(h + 1) * 128], Apre[:],
            start=True, stop=True), waits=[ev_am, ev_vpre, p.use("psD")], inc="te")
        ev_sq = p.do("sc", lambda s: s.activation(osqp[:], psD[:, 0:32], AF.Square),
                     waits=[ev_o, p.use("osqp")], inc="ac")
        ev_ss = p.do("te", lambda t: t.matmul(
            psR1[0:1, 0:32], onesb[:], osqp[:], start=True, stop=True),
            waits=[ev_sq, p.use("psR1")], inc="te")
        p.release("osqp", ev_ss)
        ev_t = p.do("ve", lambda v: v.tensor_scalar(
            mrow[0:1, 0:32], psR1[0:1, 0:32], 1.0 / 128.0, 1e-5,
            ALU.mult, ALU.add), waits=[ev_ss, p.use("mrow")], inc="ve")
        p.release("psR1", ev_t)
        ev_rc = p.do("ve", lambda v: v.reciprocal(
            mrow[0:1, 0:32], mrow[0:1, 0:32]), waits=[ev_t], inc="ve")
        ev_r = p.do("sc", lambda s: s.activation(
            rrow[0:1, 0:32], mrow[0:1, 0:32], AF.Sqrt),
            waits=[ev_rc, p.use("rrow")], inc="ac")
        p.release("mrow", ev_r)
        ev_bc = p.do("te", lambda t: t.matmul(
            psE[:, 0:32], onesr_f[:], rrow[0:1, 0:32], start=True, stop=True),
            waits=[ev_r, p.use("psE")], inc="te")
        p.release("rrow", ev_bc)
        ev_rb = p.do("ve", lambda v: v.tensor_copy(rbp[:], psE[:, 0:32]),
                     waits=[ev_bc, p.use("rbp")], inc="ve")
        p.release("psE", ev_rb)
        ev_op = p.do("ve", lambda v, h=h: v.scalar_tensor_tensor(
            opre[:, h * 32:(h + 1) * 32], psD[:, 0:32], rmsw_s[:],
            rbp[:], ALU.mult, ALU.mult), waits=[ev_rb], inc="ve")
        p.release("psD", ev_op)
        p.release("rbp", ev_op)
    ev_opre = p.mark("ve")
    phase_barrier()
    b2h_st.close()

    b2w = contextlib.ExitStack()
    woh = [mksb(b2w, "woh0", [128, KD * 128], BF16),
           mksb(b2w, "woh1", [128, KD * 128], BF16)]
    evo = {0: gdma(woh[0][:].rearrange("p (k c) -> p k c", k=KD),
                   WoT[:, 0:128].rearrange("(k p) c -> p k c", p=128),
                   stream="woh0")}
    ev_ccx = None
    for mo in range(KD):
        par = mo % 2
        if mo + 1 < KD:
            evo[mo + 1] = gdma(woh[(mo + 1) % 2][:].rearrange("p (k c) -> p k c", k=KD),
                               WoT[:, (mo + 1) * 128:(mo + 2) * 128]
                               .rearrange("(k p) c -> p k c", p=128),
                               waits=[p.use("woh%d" % ((mo + 1) % 2))],
                               stream="woh%d" % ((mo + 1) % 2))
        for hh in range(H):
            p.do("te", lambda t, mo=mo, hh=hh, par=par: t.matmul(
                psC[:, 0:16], woh[par][:, hh * 128:(hh + 1) * 128],
                opre[:, hh * 32 + 1: hh * 32 + 17],
                start=(hh == 0), stop=(hh == H - 1)),
                waits=([evo[mo], ev_opre, p.use("psC")] if hh == 0 else ()))
        ev = p.do("te", None, inc="te")
        ev_x2p = p.do("ve", lambda v, mo=mo: v.tensor_tensor(
            x2pre[:, mo * 16:(mo + 1) * 16], psC[:, 0:16],
            x1pre[:, mo * 32 + 1: mo * 32 + 17], ALU.add),
            waits=[ev], inc="ve")
        p.release("psC", ev_x2p)
        p.release("woh%d" % par, p.mark("te"))
    ev_x2pre = p.mark("ve")
    ev_ccx = sdma(ccin.ap()[2048:2176, :], x2pre[:], waits=[ev_x2pre],
                  stream="ci")
    phase_barrier()
    b2w.close()
    b2aux.close()
    ev_cc = p.do("gp", lambda g: g.collective_compute(
        "AllGather", ALU.bypass, replica_groups=[list(range(NCORE))],
        ins=[ccin.ap().opt()], outs=[ccout.ap().opt()]),
        waits=[p.mark("Dci")], inc="cc")

    oall = mksb(attn, "oall", [128, H * TPC], BF16)
    Sprefb = mksb(attn, "Sprefb", [128, H * 256], BF16)
    b3p = contextlib.ExitStack()
    acc = mksb(b3p, "acc", [128, 16 * 128], F32)
    stg = [mksb(b3p, "stg0", [128, 16 * 128], F32),
           mksb(b3p, "stg1", [128, 16 * 128], F32)]
    ccov = ccout.ap()
    ev_acc = None
    for i in range(NCORE):
        ev_ld = sdma(stg[i % 2][:].rearrange("p (g d) -> p g d", g=16),
                     ccov[i * 2176:i * 2176 + 2048, :]
                     .rearrange("(g p) d -> p g d", p=128),
                     waits=[ev_cc, p.use("stg%d" % (i % 2))],
                     stream="stg%d" % (i % 2))
        if i == 0:
            ev_acc = p.do("ve", lambda v, i=i: v.tensor_scalar(
                acc[:], stg[0][:], prefm[:, 0:1], None, ALU.mult),
                waits=[ev_ld], inc="ve")
        else:
            ev_acc = p.do("ve", lambda v, i=i: v.scalar_tensor_tensor(
                acc[:], stg[i % 2][:], prefm[:, i:i + 1], acc[:],
                ALU.mult, ALU.add), waits=[ev_ld, ev_acc], inc="ve")
        p.release("stg%d" % (i % 2), ev_acc)
    ev_spf = p.do("ve", lambda v: v.tensor_copy(Sprefb[:], acc[:]),
                  waits=[ev_acc], inc="ve")
    phase_barrier()
    b3p.close()

    b3h = contextlib.ExitStack()
    wqh2 = [mksb(b3h, "wq20", [128, KD * 128], BF16),
            mksb(b3h, "wq21", [128, KD * 128], BF16)]
    qT2 = mksb(b3h, "qT2", [128, TPC], BF16)
    qf = mksb(b3h, "qf", [128, 2 * TPC], BF16)
    kf = mksb(b3h, "kf", [128, 2 * TPC], BF16)
    Am = mksb(b3h, "Am", [128, 2 * 256], BF16)
    osq = mksb(b3h, "osq", [128, 256], BF16)
    rb3 = mksb(b3h, "rb3", [128, 256], F32)
    evq2 = {0: gdma(wqh2[0][:].rearrange("p (k c) -> p k c", k=KD),
                    WqT[:, 0:128].rearrange("(k p) c -> p k c", p=128),
                    stream="wq20")}
    for h in range(H):
        par = h % 2
        if h + 1 < H:
            evq2[h + 1] = gdma(wqh2[(h + 1) % 2][:].rearrange("p (k c) -> p k c", k=KD),
                               WqT[:, (h + 1) * 128:(h + 2) * 128]
                               .rearrange("(k p) c -> p k c", p=128),
                               waits=[p.use("wq2%d" % ((h + 1) % 2))],
                               stream="wq2%d" % ((h + 1) % 2))
        for w in range(4):
            bank, bn = (psA, "psA") if w % 2 == 0 else (psB, "psB")
            for k in range(KD):
                p.do("te", lambda t, h=h, k=k, w=w, par=par, bank=bank: t.matmul(
                    bank[:, 0:512], wqh2[par][:, k * 128:(k + 1) * 128],
                    xn2[:, k * TPC + w * 512: k * TPC + (w + 1) * 512],
                    start=(k == 0), stop=(k == KD - 1)),
                    waits=([evq2[h], p.use(bn)] if k == 0 else ()))
            ev = p.do("te", None, inc="te")
            ev_c = p.do("ve", lambda v, w=w, bank=bank: v.tensor_copy(
                qT2[:, w * 512:(w + 1) * 512], bank[:, 0:512]),
                waits=[ev, p.use("qT2") if w == 0 else None], inc="ve")
            p.release(bn, ev_c)
        ev_qT = p.mark("ve")
        p.release("wq2%d" % par, p.mark("te"))
        for w in range(4):
            bank, bn = (psC, "psC") if w % 2 == 0 else (psD, "psD")
            ev = p.do("te", lambda t, w=w, bank=bank: t.matmul(
                bank[:, 0:512], fmqT_s[:], qT2[:, w * 512:(w + 1) * 512],
                start=True, stop=True), waits=[ev_qT, p.use(bn)], inc="te")
            p.do("sc", lambda s, w=w, bank=bank: s.activation(
                qf[:, w * 512:(w + 1) * 512], bank[:, 0:512],
                AF.Exp, bias=fmqb_m[:]),
                waits=[ev, p.use("qf") if w == 0 else None])
            ev_x = p.do("sc", lambda s, w=w, bank=bank: s.activation(
                qf[:, TPC + w * 512: TPC + (w + 1) * 512], bank[:, 0:512],
                AF.Exp, bias=fmqb_mn[:], scale=-1.0), inc="ac")
            p.release(bn, ev_x)
        p.release("qT2", p.mark("te"))
        for w in range(4):
            p.do("sc", lambda s, h=h, w=w: s.activation(
                kf[:, w * 512:(w + 1) * 512],
                ykall[:, h * TPC + w * 512: h * TPC + (w + 1) * 512],
                AF.Exp, bias=fmkb_s[:]),
                waits=[p.use("kf")] if w == 0 else ())
            p.do("sc", lambda s, h=h, w=w: s.activation(
                kf[:, TPC + w * 512: TPC + (w + 1) * 512],
                ykall[:, h * TPC + w * 512: h * TPC + (w + 1) * 512],
                AF.Exp, bias=fmkb_n[:], scale=-1.0))
        ev_kf = p.do("sc", None, inc="ac")
        for sci in range(NSC):
            obank, obn = (psE, "psE") if sci % 2 == 0 else (psF, "psF")
            for ktc in range(2):
                bank, bn = (psA, "psA") if ktc == 0 else (psB, "psB")
                for fc in range(2):
                    p.do("te", lambda t, sci=sci, ktc=ktc, fc=fc: t.matmul(
                        (psA if ktc == 0 else psB)[:, 0:256],
                        kf[:, fc * TPC + (sci * 2 + ktc) * 128:
                           fc * TPC + (sci * 2 + ktc) * 128 + 128],
                        qf[:, fc * TPC + sci * 256: fc * TPC + sci * 256 + 256],
                        start=(fc == 0), stop=(fc == 1)),
                        waits=([ev_kf, p.mark("ac"), p.use(bn)] if fc == 0 else ()))
                ev = p.do("te", None, inc="te")
                ev_c = p.do("ve", lambda v, ktc=ktc, bank=bank: v.tensor_tensor(
                    Am[:, ktc * 256:(ktc + 1) * 256], bank[:, 0:256],
                    maskA[:, ktc * 256:(ktc + 1) * 256], ALU.mult),
                    waits=[ev, p.use("Am") if ktc == 0 else None], inc="ve")
            ev_am = ev_c
            p.release("psA", ev_am)
            p.release("psB", ev_am)
            first = True
            for ktc in range(2):
                tc = sci * 2 + ktc
                p.do("te", lambda t, h=h, tc=tc, ktc=ktc, obank=obank, first=first: t.matmul(
                    obank[:, 0:256],
                    vall[:, tc * 1024 + h * 128: tc * 1024 + (h + 1) * 128],
                    Am[:, ktc * 256:(ktc + 1) * 256],
                    start=first, stop=False),
                    waits=([ev_am, p.use(obn)] if first else [ev_am]))
                first = False
            if sci > 0:
                for fc in range(2):
                    p.do("te", lambda t, h=h, sci=sci, fc=fc, obank=obank: t.matmul(
                        obank[:, 0:256],
                        Scum[:, (h * 7 + sci - 1) * 256 + fc * 128:
                             (h * 7 + sci - 1) * 256 + (fc + 1) * 128],
                        qf[:, fc * TPC + sci * 256: fc * TPC + sci * 256 + 256],
                        start=False, stop=False))
            for fc in range(2):
                p.do("te", lambda t, h=h, sci=sci, fc=fc, obank=obank: t.matmul(
                    obank[:, 0:256],
                    Sprefb[:, h * 256 + fc * 128: h * 256 + (fc + 1) * 128],
                    qf[:, fc * TPC + sci * 256: fc * TPC + sci * 256 + 256],
                    start=False, stop=(fc == 1)),
                    waits=[ev_spf] if (fc == 0 and sci == 0) else ())
            ev_o = p.do("te", None, inc="te")
            ev_sq = p.do("sc", lambda s, obank=obank: s.activation(
                osq[:], obank[:, 0:256], AF.Square),
                waits=[ev_o, p.use("osq")], inc="ac")
            ev_ss = p.do("te", lambda t: t.matmul(
                psR1[0:1, 0:256], onesb[:], osq[:], start=True, stop=True),
                waits=[ev_sq, p.use("psR1")], inc="te")
            p.release("osq", ev_ss)
            ev_t = p.do("ve", lambda v: v.tensor_scalar(
                mrow[0:1, 0:256], psR1[0:1, 0:256], 1.0 / 128.0, 1e-5,
                ALU.mult, ALU.add), waits=[ev_ss, p.use("mrow")], inc="ve")
            p.release("psR1", ev_t)
            ev_rc = p.do("ve", lambda v: v.reciprocal(
                mrow[0:1, 0:256], mrow[0:1, 0:256]), waits=[ev_t], inc="ve")
            ev_r = p.do("sc", lambda s: s.activation(
                rrow[0:1, 0:256], mrow[0:1, 0:256], AF.Sqrt),
                waits=[ev_rc, p.use("rrow")], inc="ac")
            p.release("mrow", ev_r)
            ev_bc = p.do("te", lambda t: t.matmul(
                psC[:, 0:256], onesr_f[:], rrow[0:1, 0:256],
                start=True, stop=True), waits=[ev_r, p.use("psC")], inc="te")
            p.release("rrow", ev_bc)
            ev_rb = p.do("ve", lambda v: v.tensor_copy(rb3[:], psC[:, 0:256]),
                         waits=[ev_bc, p.use("rb3")], inc="ve")
            p.release("psC", ev_rb)
            ev_oo = p.do("ve", lambda v, h=h, sci=sci, obank=obank: v.scalar_tensor_tensor(
                oall[:, h * TPC + sci * 256: h * TPC + (sci + 1) * 256],
                obank[:, 0:256], rmsw_s[:], rb3[:], ALU.mult, ALU.mult),
                waits=[ev_rb], inc="ve")
            p.release("rb3", ev_oo)
            p.release(obn, ev_oo)
            p.release("Am", p.mark("te"))
        p.release("qf", p.mark("te"))
        p.release("kf", p.mark("te"))
    ev_oall = p.mark("ve")
    phase_barrier()
    b3h.close()

    b3w = contextlib.ExitStack()
    woh2 = [mksb(b3w, "wo20", [128, KD * 128], BF16),
            mksb(b3w, "wo21", [128, KD * 128], BF16)]
    x1r = mksb(b3w, "x1r", [128, TPC], F32)
    evo2 = {0: gdma(woh2[0][:].rearrange("p (k c) -> p k c", k=KD),
                    WoT[:, 0:128].rearrange("(k p) c -> p k c", p=128),
                    stream="wo20")}
    ev_x2st = None
    for mo in range(KD):
        par = mo % 2
        if mo + 1 < KD:
            evo2[mo + 1] = gdma(woh2[(mo + 1) % 2][:].rearrange("p (k c) -> p k c", k=KD),
                                WoT[:, (mo + 1) * 128:(mo + 2) * 128]
                                .rearrange("(k p) c -> p k c", p=128),
                                waits=[p.use("wo2%d" % ((mo + 1) % 2))],
                                stream="wo2%d" % ((mo + 1) % 2))
        ev_x1r = sdma(x1r[:], x1sp[mo * 128:(mo + 1) * 128, :],
                      waits=[p.use("x1r")], stream="x1r")
        for w in range(4):
            bank, bn = (psA, "psA") if w % 2 == 0 else (psB, "psB")
            for hh in range(H):
                p.do("te", lambda t, mo=mo, hh=hh, w=w, par=par, bank=bank: t.matmul(
                    bank[:, 0:512], woh2[par][:, hh * 128:(hh + 1) * 128],
                    oall[:, hh * TPC + w * 512: hh * TPC + (w + 1) * 512],
                    start=(hh == 0), stop=(hh == H - 1)),
                    waits=([evo2[mo], ev_oall, p.use(bn)] if hh == 0 else ()))
            ev = p.do("te", None, inc="te")
            ev_e = p.do("ve", lambda v, w=w, bank=bank: v.tensor_tensor(
                x1r[:, w * 512:(w + 1) * 512], bank[:, 0:512],
                x1r[:, w * 512:(w + 1) * 512], ALU.add),
                waits=[ev, ev_x1r], inc="ve")
            p.release(bn, ev_e)
        ev_x2st = sdma(x2sp[mo * 128:(mo + 1) * 128, :], x1r[:],
                       waits=[p.mark("ve")], stream="st")
        p.release("x1r", ev_x2st)
        p.release("wo2%d" % par, p.mark("te"))
    phase_barrier()
    b3w.close()
    attn.close()

    b4 = contextlib.ExitStack()
    xp2f = mksb(b4, "xp2f", [128, KD * 16], F32)
    xp2b = mksb(b4, "xp2b", [128, KD * 16], BF16)
    ev_g1 = sdma(xp2f[:], ccov[2048:2176, :], waits=[ev_cc], stream="xp")
    ev_g2 = gdma(xp2b[:], ccov[2048:2176, :], waits=[ev_cc], stream="xpb")
    P16b = mksb(b4, "P16b", [128, KD * 16], BF16)
    ev_p16b = ln_subtile(lambda k: xp2f[:, k * 16:(k + 1) * 16],
                         lambda k: xp2b[:, k * 16:(k + 1) * 16],
                         lambda k: P16b[:, k * 16:(k + 1) * 16],
                         ev_g1, ev_g2, 16)
    ev_loraB, loraB = prep_lora(ev_p16b, b4, P16b, "b")
    ev_out_store, _ = ffn_phase(b4, x2sp, outT, ev_loraB, loraB, False,
                                None, None, None, sfx="b")
    b4.close()

    ws = [p.mark(s) for s in list(p.semv) if s.startswith("D")]
    p.do("sy", None, waits=ws)
    p.do("gp", None, waits=ws)

    with nc.Block() as blk:
        blk.tensor(lambda t: p.replay("te", t))
        blk.vector(lambda v: p.replay("ve", v))
        blk.scalar(lambda s: p.replay("sc", s))
        blk.gpsimd(lambda g: p.replay("gp", g))
        blk.sync(lambda s: p.replay("sy", s))
    psbank.close()
    top.close()
    return nc


_NC_CACHE = [None]


def kernel(L0_attn_x, L0_attn_Wq, L0_attn_Wk, L0_attn_Wv, L0_attn_Wo,
           L0_attn_fmq_w, L0_attn_fmq_b, L0_attn_fmk_w, L0_attn_fmk_b,
           L0_attn_rms_w, L0_ffn_W1, L0_ffn_b1, L0_ffn_W2, L0_ffn_b2,
           L0_ffn_scale):
    f32 = np.float32
    xTf = np.ascontiguousarray(np.asarray(L0_attn_x, dtype=f32)[0].T)
    com = dict(
        xTpre=np.ascontiguousarray(xTf[:, 0:32]),
        WqT=np.ascontiguousarray(np.asarray(L0_attn_Wq, f32).T),
        WkT=np.ascontiguousarray(np.asarray(L0_attn_Wk, f32).T),
        WvT=np.ascontiguousarray(np.asarray(L0_attn_Wv, f32).T),
        WoT=np.ascontiguousarray(np.asarray(L0_attn_Wo, f32).T),
        W1T=np.ascontiguousarray(np.asarray(L0_ffn_W1, f32).T),
        W2T=np.ascontiguousarray(np.asarray(L0_ffn_W2, f32).T),
        fmqT=np.ascontiguousarray(np.asarray(L0_attn_fmq_w, f32).T),
        fmkT=np.ascontiguousarray(np.asarray(L0_attn_fmk_w, f32).T),
        fmqb_dr=np.asarray(L0_attn_fmq_b, f32).reshape(128, 1).copy(),
        fmkb_dr=np.asarray(L0_attn_fmk_b, f32).reshape(128, 1).copy(),
        fmkb_r=np.asarray(L0_attn_fmk_b, f32).reshape(1, 128).copy(),
        rmsw_dr=np.asarray(L0_attn_rms_w, f32).reshape(128, 1).copy(),
        b1c_dr=np.ascontiguousarray(np.asarray(L0_ffn_b1, f32).reshape(MH, 128).T),
        b2c_dr=np.ascontiguousarray(np.asarray(L0_ffn_b2, f32).reshape(KD, 128).T),
        scl_dr=np.full((128, 1), float(np.asarray(L0_ffn_scale)), f32),
        maskA_dr=np.concatenate([np.triu(np.ones((128, 256), f32)),
                              np.triu(np.ones((128, 256), f32), 128)], axis=1),
        maskup_dr=np.eye(4, dtype=f32),
        eye16r_dr=np.eye(16, dtype=f32).reshape(1, 256).copy(),
        eye16m_dr=np.eye(16, dtype=f32),
        identd=np.eye(128, dtype=f32),
        cpoold=np.tile(np.array([1e-5, -0.5, 0.5, 1.0 / 1024.0, 1.0 / 128.0,
                                 -1.0, 2.0, 1.0], f32), (128, 1)),
    )
    in_maps = []
    for c in range(NCORE):
        pm = np.zeros((128, NCORE), f32)
        pm[:, :c] = 1.0
        m = dict(com)
        m["xT"] = np.ascontiguousarray(xTf[:, c * TPC:(c + 1) * TPC])
        m["prefm_dr"] = pm
        in_maps.append(m)

    if _NC_CACHE[0] is None:
        _NC_CACHE[0] = build_program()
    nc = _NC_CACHE[0]
    res = run_bass_kernel_spmd(nc, in_maps, core_ids=list(range(NCORE)))
    parts = [np.asarray(r["outT"]) for r in res.results]
    out = np.concatenate(parts, axis=1).T.copy()
    return out.reshape(1, N, D).astype(np.float32)


# version 10 (final)
# speedup vs baseline: 1.0479x; 1.0479x over previous
import math
import contextlib
import numpy as np

import concourse.bass as bass
import concourse.mybir as mybir
from concourse.bass_utils import run_bass_kernel_spmd

F32 = mybir.dt.float32
BF16 = mybir.dt.bfloat16
AF = mybir.ActivationFunctionType
ALU = mybir.AluOpType

DBG = False

NCORE = 8
N = 16384
D = 1024
TPC = N // NCORE
KD = D // 128
H = 8
DH = 4096
MH = DH // 128
SC = 256
NSC = TPC // SC
FT = 1024
NFT = TPC // FT
LN16 = float(math.log(16.0))

ENGS = ("te", "ve", "sc", "gp", "sy")


class Prog:

    def __init__(self, nc):
        self.nc = nc
        self.q = {e: [] for e in ENGS}
        self.semh = {}
        self.semv = {}
        self.guard = {}

    def sem(self, name, stack):
        self.semh[name] = stack.enter_context(self.nc.semaphore(name))
        self.semv[name] = 0

    def dsem(self, name):
        if name not in self.semh:
            self.semh[name] = self.nc.ctx.enter_context(self.nc.semaphore(name))
            self.semv[name] = 0
        return name

    def do(self, eng, builder, waits=(), inc=None, inc_n=1):
        ws = tuple(w for w in waits if w is not None)
        ev = None
        if inc is not None:
            self.semv[inc] += inc_n
            ev = (inc, self.semv[inc])
        self.q[eng].append((ws, builder, inc, inc_n))
        return ev

    def mark(self, semname):
        return (semname, self.semv[semname])

    def use(self, res):
        return self.guard.get(res)

    def release(self, res, ev):
        self.guard[res] = ev

    def replay(self, eng, obj):
        maxw = {}
        last = None
        last_had_inc = False
        for waits, builder, inc, inc_n in self.q[eng]:
            for s, v in waits:
                if v > maxw.get(s, 0):
                    obj.wait_ge(self.semh[s], v)
                    maxw[s] = v
            if builder is None:
                if inc is not None:
                    assert last is not None and not last_had_inc, \
                        "marker would double-inc"
                    last.then_inc(self.semh[inc], inc_n)
                    last_had_inc = True
                continue
            ins = builder(obj)
            last = ins
            last_had_inc = inc is not None
            if inc is not None:
                ins.then_inc(self.semh[inc], inc_n)


def build_program():
    nc = bass.Bass("TRN2", target_bir_lowering=False)
    dt = nc.dram_tensor

    xT = dt("xT", [D, TPC], F32, kind="ExternalInput").ap()
    xTpre = dt("xTpre", [D, 32], F32, kind="ExternalInput").ap()
    WqT = dt("WqT", [D, D], F32, kind="ExternalInput").ap()
    WkT = dt("WkT", [D, D], F32, kind="ExternalInput").ap()
    WvT = dt("WvT", [D, D], F32, kind="ExternalInput").ap()
    WoT = dt("WoT", [D, D], F32, kind="ExternalInput").ap()
    W1T = dt("W1T", [D, DH], F32, kind="ExternalInput").ap()
    W2T = dt("W2T", [DH, D], F32, kind="ExternalInput").ap()
    fmqT_d = dt("fmqT", [128, 128], F32, kind="ExternalInput").ap()
    fmkT_d = dt("fmkT", [128, 128], F32, kind="ExternalInput").ap()
    fmqb_d = dt("fmqb_dr", [128, 1], F32, kind="ExternalInput").ap()
    fmkb_d = dt("fmkb_dr", [128, 1], F32, kind="ExternalInput").ap()
    fmkbr_d = dt("fmkb_r", [1, 128], F32, kind="ExternalInput").ap()
    rmsw_d = dt("rmsw_dr", [128, 1], F32, kind="ExternalInput").ap()
    b1c_d = dt("b1c_dr", [128, MH], F32, kind="ExternalInput").ap()
    b2c_d = dt("b2c_dr", [128, KD], F32, kind="ExternalInput").ap()
    scl_d = dt("scl_dr", [128, 1], F32, kind="ExternalInput").ap()
    prefm_d = dt("prefm_dr", [128, NCORE], F32, kind="ExternalInput").ap()
    maskA_d = dt("maskA_dr", [128, 512], F32, kind="ExternalInput").ap()
    maskup_d = dt("maskup_dr", [4, 4], F32, kind="ExternalInput").ap()
    eye16r_d = dt("eye16r_dr", [1, 256], F32, kind="ExternalInput").ap()
    eye16m_d = dt("eye16m_dr", [16, 16], F32, kind="ExternalInput").ap()
    ident_d = dt("identd", [128, 128], F32, kind="ExternalInput").ap()
    cpool_d = dt("cpoold", [128, 8], F32, kind="ExternalInput").ap()

    outT = dt("outT", [D, TPC], F32, kind="ExternalOutput").ap()

    dkind = "ExternalOutput" if DBG else "Internal"
    x1sp = dt("x1sp", [D, TPC], F32, kind=dkind).ap()
    x2sp = dt("x2sp", [D, TPC], F32, kind=dkind).ap()
    ccin = dt("ccin", [2176, 128], F32, kind=dkind)
    ccout = dt("ccout", [NCORE * 2176, 128], F32, addr_space="Shared")
    if DBG:
        dbg1 = dt("dbg1", [128, 384], F32, kind="ExternalOutput").ap()
        dbg2 = dt("dbg2", [128, 512], BF16, kind="ExternalOutput").ap()

    top = contextlib.ExitStack()

    def mksb(stack, name, shape, dtyp):
        return stack.enter_context(nc.sbuf_tensor(name, shape, dtyp))

    def mkpt(stack, name, shape):
        return stack.enter_context(nc.psum_tensor(name, list(shape), F32))

    p = Prog(nc)
    for s in ("te", "ve", "ac", "cc", "gp"):
        p.sem(s, top)

    def gdma(dst, src, waits=(), stream="csg"):
        return p.do("gp", lambda g, d=dst, s=src: g.dma_start(out=d, in_=s),
                    waits=waits, inc=p.dsem("D" + stream), inc_n=16)

    def sdma(dst, src, waits=(), stream="css"):
        return p.do("sy", lambda s_, d=dst, s=src: s_.dma_start(out=d, in_=s),
                    waits=waits, inc=p.dsem("D" + stream), inc_n=16)

    def phase_barrier():
        ws = [p.mark("te"), p.mark("ve"), p.mark("ac")]
        ws += [p.mark(s) for s in list(p.semv) if s.startswith("D")]
        for e in ENGS:
            p.do(e, None, waits=ws)

    cpool = mksb(top, "cpool", [128, 8], F32)
    onesb = mksb(top, "onesb", [128, 1], BF16)
    onesr_f = mksb(top, "onesr_f", [1, 128], F32)
    onesr_b = mksb(top, "onesr_b", [1, 128], BF16)
    identb = mksb(top, "identb", [128, 128], BF16)
    identf = mksb(top, "identf", [128, 128], F32)
    maskA = mksb(top, "maskA", [128, 512], F32)
    maskup = mksb(top, "maskup", [4, 4], F32)
    eye16r = mksb(top, "eye16r", [1, 256], BF16)
    eye16m = mksb(top, "eye16m", [16, 16], BF16)
    fmqT_s = mksb(top, "fmqT_s", [128, 128], BF16)
    fmkT_s = mksb(top, "fmkT_s", [128, 128], BF16)
    fmqb_s = mksb(top, "fmqb_s", [128, 1], F32)
    fmkb_s = mksb(top, "fmkb_s", [128, 1], F32)
    fmkbr_s = mksb(top, "fmkbr_s", [1, 128], BF16)
    fmqb_m = mksb(top, "fmqb_m", [128, 1], F32)
    fmqb_mn = mksb(top, "fmqb_mn", [128, 1], F32)
    fmkb_n = mksb(top, "fmkb_n", [128, 1], F32)
    rmsw_s = mksb(top, "rmsw_s", [128, 1], F32)
    b1c = mksb(top, "b1c", [128, MH], F32)
    b2h = mksb(top, "b2h", [128, KD], F32)
    scl = mksb(top, "scl", [128, 1], F32)
    sclh = mksb(top, "sclh", [128, 1], F32)
    prefm = mksb(top, "prefm", [128, NCORE], F32)
    x1pre = mksb(top, "x1pre", [128, KD * 32], F32)
    xn2pre = mksb(top, "xn2pre", [128, KD * 32], BF16)
    mrow = mksb(top, "mrow", [1, 512], F32)
    rrow = mksb(top, "rrow", [1, 512], F32)
    msq = mksb(top, "msq", [1, 512], F32)
    xsqt = mksb(top, "xsqt", [128, 512], BF16)
    lntmp = mksb(top, "lntmp", [128, 512], F32)

    for j, v in enumerate([1e-5, -0.5, 0.5, 1.0 / 1024.0, 1.0 / 128.0,
                           -1.0, 2.0, 1.0]):
        nc.const_aps.register(F32, float(v), cpool[:, j:j + 1])

    psbank = contextlib.ExitStack()
    psA = mkpt(psbank, "psA", (128, 512))
    psB = mkpt(psbank, "psB", (128, 512))
    psC = mkpt(psbank, "psC", (128, 512))
    psD = mkpt(psbank, "psD", (128, 512))
    psE = mkpt(psbank, "psE", (128, 512))
    psF = mkpt(psbank, "psF", (128, 512))
    psR1 = mkpt(psbank, "psR1", (1, 512))
    psR2 = mkpt(psbank, "psR2", (1, 512))

    sdma(cpool[:], cpool_d)
    sdma(maskA[:], maskA_d)
    sdma(maskup[:], maskup_d)
    sdma(fmqb_s[:], fmqb_d)
    sdma(fmkb_s[:], fmkb_d)
    sdma(rmsw_s[:], rmsw_d)
    sdma(b1c[:], b1c_d)
    sdma(b2h[:], b2c_d)
    sdma(scl[:], scl_d)
    sdma(prefm[:], prefm_d)
    gdma(identb[:], ident_d)
    sdma(identf[:], ident_d)
    gdma(eye16r[:], eye16r_d)
    gdma(eye16m[:], eye16m_d)
    gdma(fmqT_s[:], fmqT_d)
    gdma(fmkT_s[:], fmkT_d)
    gdma(fmkbr_s[:], fmkbr_d)
    p.do("gp", lambda g: g.memset(onesb[:], 1.0))
    p.do("gp", lambda g: g.memset(onesr_f[:], 1.0))
    ev_ms = p.do("gp", lambda g: g.memset(onesr_b[:], 1.0), inc="gp")

    b1 = contextlib.ExitStack()
    P16a = mksb(b1, "P16a", [128, KD * 16], BF16)
    xpre_f = mksb(b1, "xpre_f", [128, KD * 32], F32)
    xpre_b = mksb(b1, "xpre_b", [128, KD * 32], BF16)
    xnpre = mksb(b1, "xnpre", [128, KD * 32], BF16)
    sdma(xpre_f[:].rearrange("p (c t) -> p c t", c=KD),
         xTpre.rearrange("(c p) t -> p c t", p=128))
    gdma(xpre_b[:].rearrange("p (c t) -> p c t", c=KD),
         xTpre.rearrange("(c p) t -> p c t", p=128))
    ev_cf = p.mark("Dcss")
    ev_cb = p.mark("Dcsg")
    ev_pf, ev_pb = ev_cf, ev_cb

    p.do("ve", lambda v: v.tensor_scalar(fmqb_m[:], fmqb_s[:], 1.0, -LN16,
                                         ALU.mult, ALU.add), waits=[ev_cf])
    p.do("ve", lambda v: v.tensor_scalar(fmqb_mn[:], fmqb_s[:], -1.0, -LN16,
                                         ALU.mult, ALU.add))
    p.do("ve", lambda v: v.tensor_scalar(fmkb_n[:], fmkb_s[:], -1.0, None, ALU.mult))
    p.do("ve", lambda v: v.tensor_scalar(b2h[:], b2h[:], 0.5, None, ALU.mult))
    ev_dc = p.do("ve", lambda v: v.tensor_scalar(sclh[:], scl[:], 0.5, None,
                                                 ALU.mult), inc="ve")

    def ln_subtile(src_f, src_b, xn_out, ev_f, ev_b, W):
        for k in range(KD):
            p.do("te", lambda t, k=k: t.matmul(
                psR1[0:1, 0:W], onesb[:], src_b(k),
                start=(k == 0), stop=(k == KD - 1)),
                waits=([ev_b, p.use("psR1")] if k == 0 else ()))
        ev_m = p.do("te", None, inc="te")
        for k in range(KD):
            ev_sq = p.do("sc", lambda s, k=k: s.activation(
                xsqt[:, 0:W], src_b(k), AF.Square),
                waits=([ev_b, p.use("xsqt")] if k == 0 else [p.use("xsqt")]),
                inc="ac")
            ev_mm = p.do("te", lambda t, k=k: t.matmul(
                psR2[0:1, 0:W], onesb[:], xsqt[:, 0:W],
                start=(k == 0), stop=(k == KD - 1)),
                waits=([ev_sq, p.use("psR2")] if k == 0 else [ev_sq]),
                inc="te")
            p.release("xsqt", ev_mm)
        ev_q = p.mark("te")
        ev_mr = p.do("ve", lambda v: v.tensor_scalar(
            mrow[0:1, 0:W], psR1[0:1, 0:W], 1.0 / 1024.0, None, ALU.mult),
            waits=[ev_m, p.use("mrow")], inc="ve")
        p.release("psR1", ev_mr)
        ev_msq = p.do("sc", lambda s: s.activation(
            msq[0:1, 0:W], mrow[0:1, 0:W], AF.Square),
            waits=[ev_mr, p.use("msq")], inc="ac")
        ev_var = p.do("ve", lambda v: v.scalar_tensor_tensor(
            rrow[0:1, 0:W], psR2[0:1, 0:W], 1.0 / 1024.0, msq[0:1, 0:W],
            ALU.mult, ALU.subtract),
            waits=[ev_q, ev_msq, p.use("rrow")], inc="ve")
        p.release("psR2", ev_var)
        p.release("msq", ev_var)
        ev_ln = p.do("sc", lambda s: s.activation(rrow[0:1, 0:W], rrow[0:1, 0:W],
                                                  AF.Ln, bias=1e-5),
                     waits=[ev_var], inc="ac")
        ev_r = p.do("sc", lambda s: s.activation(
            rrow[0:1, 0:W], rrow[0:1, 0:W], AF.Exp, scale=-0.5),
            waits=[ev_ln], inc="ac")
        p.do("te", lambda t: t.matmul(psA[:, 0:W], onesr_f[:], mrow[0:1, 0:W],
                                      start=True, stop=True),
             waits=[ev_mr, p.use("psA")])
        ev_bc = p.do("te", lambda t: t.matmul(
            psB[:, 0:W], onesr_f[:], rrow[0:1, 0:W], start=True, stop=True),
            waits=[ev_r, p.use("psB")], inc="te")
        p.release("mrow", ev_bc)
        p.release("rrow", ev_bc)
        ev_done = None
        for k in range(KD):
            ev_sub = p.do("ve", lambda v, k=k: v.tensor_tensor(
                lntmp[:, 0:W], src_f(k), psA[:, 0:W], ALU.subtract),
                waits=([ev_bc, ev_f, p.use("lntmp")] if k == 0 else [ev_done]),
                inc="ve")
            ev_done = p.do("ve", lambda v, k=k: v.tensor_tensor(
                xn_out(k), lntmp[:, 0:W], psB[:, 0:W], ALU.mult),
                waits=[ev_sub], inc="ve")
        p.release("lntmp", ev_done)
        p.release("psA", ev_done)
        p.release("psB", ev_done)
        return ev_done

    def prep_lora(ev_p16, lst, P16, sfx):
        L_tm = mksb(lst, "L_tm" + sfx, [16, 1024], BF16)
        EAZ = mksb(lst, "EAZ" + sfx, [4, 4 * 1024], BF16)
        ET = mksb(lst, "ET" + sfx, [16, 1024], BF16)
        OZ = mksb(lst, "OZ" + sfx, [128, 32 * 16], BF16)
        E2r = mksb(lst, "E2r" + sfx, [1, 4 * 1024], BF16)
        EA = mksb(lst, "EA" + sfx, [4, 1024], BF16)
        trf = mksb(lst, "trf" + sfx, [128, 16], F32)
        ev_c = None
        for k in range(KD):
            ev_f = p.do("sc", lambda s, k=k: s.copy(
                trf[:], P16[:, k * 16:(k + 1) * 16]),
                waits=[ev_p16, p.use("trf")] if k == 0 else [p.use("trf")],
                inc="ac")
            ev_t = p.do("te", lambda t, k=k: t.transpose(
                psC[0:16, 0:128], trf[:], identf[:]),
                waits=[ev_f, p.use("psC")], inc="te")
            p.release("trf", ev_t)
            ev_c = p.do("ve", lambda v, k=k: v.tensor_copy(
                L_tm[:, k * 128:(k + 1) * 128], psC[0:16, 0:128]),
                waits=[ev_t], inc="ve")
            p.release("psC", ev_c)
        ev_ltm = ev_c
        for hf in range(2):
            ev = p.do("te", lambda t, hf=hf: t.matmul(
                psC[0:4, 0:512], eye16m[:, 0:8:2],
                L_tm[:, hf * 512:(hf + 1) * 512], start=True, stop=True),
                waits=[ev_ltm, p.use("psC")], inc="te")
            ev_c = p.do("ve", lambda v, hf=hf: v.tensor_copy(
                EA[:, hf * 512:(hf + 1) * 512], psC[0:4, 0:512]),
                waits=[ev], inc="ve")
            p.release("psC", ev_c)
        ev_ea = ev_c
        for i in range(4):
            for hf in range(2):
                ev = p.do("te", lambda t, i=i, hf=hf: t.matmul(
                    psC[0:1, 0:512], eye16m[:, 8 + 2 * i:9 + 2 * i],
                    L_tm[:, hf * 512:(hf + 1) * 512], start=True, stop=True),
                    waits=[p.use("psC")], inc="te")
                ev_c = p.do("ve", lambda v, i=i, hf=hf: v.tensor_copy(
                    E2r[0:1, i * 1024 + hf * 512: i * 1024 + (hf + 1) * 512],
                    psC[0:1, 0:512]), waits=[ev], inc="ve")
                p.release("psC", ev_c)
        ev_e2 = ev_c
        ev_eaz = None
        for i in range(4):
            ev_eaz = p.do("ve", lambda v, i=i: v.tensor_scalar(
                EAZ[:, i * 1024:(i + 1) * 1024], EA[:], maskup[:, i:i + 1],
                None, ALU.mult), waits=[ev_ea], inc="ve")
        for hf in range(2):
            ilist = (0, 1) if hf == 0 else (2, 3)
            items = [(i, a2) for i in ilist for a2 in range(4)]
            for idx, (i, a2) in enumerate(items):
                r = a2 * 4 + i
                c0 = (i - ilist[0]) * 256
                p.do("te", lambda t, r=r, i=i, a2=a2, c0=c0, idx=idx: t.matmul(
                    psC[0:16, c0:c0 + 256], eye16r[0:1, r * 16:(r + 1) * 16],
                    E2r[0:1, i * 1024 + a2: i * 1024 + a2 + 1021:4],
                    start=(a2 == 0), stop=(a2 == 3)),
                    waits=([ev_e2, p.use("psC")] if idx == 0 else ()))
            ev = p.do("te", None, inc="te")
            ev_c = p.do("ve", lambda v, hf=hf: v.tensor_copy(
                ET[:, hf * 512:(hf + 1) * 512], psC[0:16, 0:512]),
                waits=[ev], inc="ve")
            p.release("psC", ev_c)
        ev_z = p.do("ve", lambda v: v.memset(OZ[:], 0.0), inc="ve")
        ev_oz = None
        for k in range(32):
            kc, a2 = k % 8, k // 8
            ev_oz = p.do("ve", lambda v, k=k, kc=kc, a2=a2: v.tensor_copy(
                OZ[:, k * 16 + a2 * 4: k * 16 + a2 * 4 + 4],
                P16[:, kc * 16 + 9: kc * 16 + 16:2]),
                waits=([ev_z, ev_p16] if k == 0 else ()), inc="ve")
        ev_lora = ev_oz
        return ev_lora, dict(L_tm=L_tm, EAZ=EAZ, ET=ET, OZ=OZ, P16=P16)

    def ffn_phase(fst, src, dst, ev_lora, lora, with_prep, ev_xnpre, xnpre_t,
                  pre_resid, sfx=""):
        P16, EAZ, ET, OZ = lora["P16"], lora["EAZ"], lora["ET"], lora["OZ"]
        xf_t = mksb(fst, "xf_t" + sfx, [128, KD * FT], F32)
        dtmp = mksb(fst, "dtmp" + sfx, [128, 2 * 512], F32)
        ptmp = mksb(fst, "ptmp" + sfx, [128, 32], F32)
        xb_t = mksb(fst, "xb_t" + sfx, [128, KD * 512], BF16)
        xn_t = mksb(fst, "xn_t" + sfx, [128, KD * FT], BF16)
        h_t = mksb(fst, "h_t" + sfx, [128, MH * FT], BF16)
        Us = mksb(fst, "Us" + sfx, [4, FT], BF16)
        Wsc = mksb(fst, "Wsc" + sfx, [16, FT], BF16)
        w1m = [mksb(fst, "w1m0" + sfx, [128, KD * 128], BF16),
               mksb(fst, "w1m1" + sfx, [128, KD * 128], BF16)]
        w2m = [mksb(fst, "w2m0" + sfx, [128, MH * 128], BF16),
               mksb(fst, "w2m1" + sfx, [128, MH * 128], BF16)]
        if with_prep:
            hpre = mksb(fst, "hpre" + sfx, [128, MH * 32], BF16)
            Uspre = mksb(fst, "Uspre" + sfx, [4, 32], BF16)
            Wscpre = mksb(fst, "Wscpre" + sfx, [16, 32], BF16)

        srcv = src.rearrange("(c p) n -> p c n", p=128)

        def load_w1(m, par):
            return gdma(w1m[par][:].rearrange("p (k c) -> p k c", k=KD),
                        W1T[:, m * 128:(m + 1) * 128]
                        .rearrange("(k p) c -> p k c", p=128),
                        waits=[p.use("w1m%d" % par + sfx)],
                        stream="w1m%d" % par + sfx)

        def load_w2(mo, par):
            return gdma(w2m[par][:].rearrange("p (k c) -> p k c", k=MH),
                        W2T[:, mo * 128:(mo + 1) * 128]
                        .rearrange("(k p) c -> p k c", p=128),
                        waits=[p.use("w2m%d" % par + sfx)],
                        stream="w2m%d" % par + sfx)

        if with_prep:
            for k in range(KD):
                p.do("te", lambda t, k=k: t.matmul(
                    psC[0:4, 0:32], P16[:, k * 16 + 1:k * 16 + 8:2],
                    xnpre_t[:, k * 32:(k + 1) * 32],
                    start=(k == 0), stop=(k == KD - 1)),
                    waits=([ev_lora, ev_xnpre, p.use("psC")] if k == 0 else ()))
            ev = p.do("te", None, inc="te")
            ev_c = p.do("ve", lambda v: v.tensor_scalar(
                Uspre[:], psC[0:4, 0:32], scl[0:4, 0:1], None, ALU.mult),
                waits=[ev], inc="ve")
            p.release("psC", ev_c)

        ev_last_store = None
        ev_up_end = None
        for t in range(NFT):
            if t > 0:
                p.do("ve", None, waits=[ev_up_end])
            ev_xf = sdma(xf_t[:].rearrange("p (c w) -> p c w", c=KD),
                         srcv[:, :, t * FT:(t + 1) * FT],
                         waits=[p.use("xf_t" + sfx)], stream="xf" + sfx)
            for w in range(FT // 512):
                ev_xb = gdma(xb_t[:].rearrange("p (c w) -> p c w", c=KD),
                             srcv[:, :, t * FT + w * 512: t * FT + (w + 1) * 512],
                             waits=[p.use("xb_t" + sfx)], stream="xb" + sfx)
                ev_xn = ln_subtile(
                    lambda k, t=t, w=w: xf_t[:, k * FT + w * 512:k * FT + (w + 1) * 512],
                    lambda k: xb_t[:, k * 512:(k + 1) * 512],
                    lambda k, t=t, w=w: xn_t[:, k * FT + w * 512:k * FT + (w + 1) * 512],
                    ev_xf, ev_xb, 512)
                p.release("xb_t" + sfx, p.mark("te"))
            ev_xn_all = p.mark("ve")
            for w in range(FT // 512):
                for k in range(KD):
                    p.do("te", lambda t_, k=k, w=w, t=t: t_.matmul(
                        psC[0:4, 0:512], P16[:, k * 16 + 1:k * 16 + 8:2],
                        xn_t[:, k * FT + w * 512: k * FT + (w + 1) * 512],
                        start=(k == 0), stop=(k == KD - 1)),
                        waits=([ev_xn_all, ev_lora, p.use("psC")] if k == 0 else ()))
                ev = p.do("te", None, inc="te")
                ev_c = p.do("ve", lambda v, w=w: v.tensor_scalar(
                    Us[:, w * 512:(w + 1) * 512], psC[0:4, 0:512],
                    scl[0:4, 0:1], None, ALU.mult), waits=[ev], inc="ve")
                p.release("psC", ev_c)
            ev_us = p.mark("ve")
            ev_w1 = load_w1(0, 0)
            evw = {0: ev_w1}
            for m in range(MH):
                par = m % 2
                if m + 1 < MH:
                    evw[m + 1] = load_w1(m + 1, (m + 1) % 2)
                for w in range(FT // 512):
                    bank, bn = (psA, "psA") if w == 0 else (psB, "psB")
                    for k in range(KD):
                        p.do("te", lambda t_, m=m, k=k, w=w, par=par, bank=bank, t=t: t_.matmul(
                            bank[:, 0:512], w1m[par][:, k * 128:(k + 1) * 128],
                            xn_t[:, k * FT + w * 512: k * FT + (w + 1) * 512],
                            start=(k == 0), stop=False),
                            waits=([evw[m], ev_us, p.use(bn)] if k == 0 else ()))
                    p.do("te", lambda t_, m=m, w=w, bank=bank: t_.matmul(
                        bank[:, 0:512],
                        EAZ[0:4, (m // 8) * 1024 + (m % 8) * 128:
                            (m // 8) * 1024 + (m % 8) * 128 + 128],
                        Us[0:4, w * 512:(w + 1) * 512], start=False, stop=True))
                    ev_mm = p.do("te", None, inc="te")
                    ev_g = p.do("sc", lambda s, m=m, w=w, bank=bank, t=t: s.activation(
                        h_t[:, m * FT + w * 512: m * FT + (w + 1) * 512],
                        bank[:, 0:512], AF.Gelu, bias=b1c[:, m:m + 1]),
                        waits=[ev_mm], inc="ac")
                    p.release(bn, ev_g)
                p.release("w1m%d" % par + sfx, p.mark("te"))
                if with_prep and t == 0:
                    for k in range(KD):
                        p.do("te", lambda t_, m=m, k=k, par=par: t_.matmul(
                            psC[:, 0:32], w1m[par][:, k * 128:(k + 1) * 128],
                            xnpre_t[:, k * 32:(k + 1) * 32],
                            start=(k == 0), stop=False),
                            waits=[p.use("psC")] if k == 0 else ())
                    p.do("te", lambda t_, m=m: t_.matmul(
                        psC[:, 0:32],
                        EAZ[0:4, (m // 8) * 1024 + (m % 8) * 128:
                            (m // 8) * 1024 + (m % 8) * 128 + 128],
                        Uspre[0:4, :], start=False, stop=True))
                    ev_mm = p.do("te", None, inc="te")
                    ev_g = p.do("sc", lambda s, m=m: s.activation(
                        hpre[:, m * 32:(m + 1) * 32], psC[:, 0:32],
                        AF.Gelu, bias=b1c[:, m:m + 1]), waits=[ev_mm], inc="ac")
                    p.release("psC", ev_g)
                    p.release("w1m%d" % par + sfx, p.mark("te"))
            ev_up_end = p.mark("te")
            ev_h = p.mark("ac")
            for w in range(FT // 512):
                for k in range(MH):
                    p.do("te", lambda t_, k=k, w=w, t=t: t_.matmul(
                        psC[0:16, 0:512], OZ[:, k * 16:(k + 1) * 16],
                        h_t[:, k * FT + w * 512: k * FT + (w + 1) * 512],
                        start=(k == 0), stop=(k == MH - 1)),
                        waits=([ev_h, p.use("psC")] if k == 0 else ()))
                ev = p.do("te", None, inc="te")
                ev_c = p.do("ve", lambda v, w=w: v.tensor_scalar(
                    Wsc[:, w * 512:(w + 1) * 512], psC[0:16, 0:512],
                    scl[0:16, 0:1], None, ALU.mult), waits=[ev], inc="ve")
                p.release("psC", ev_c)
            ev_wsc = p.mark("ve")
            if with_prep and t == 0:
                for k in range(MH):
                    p.do("te", lambda t_, k=k: t_.matmul(
                        psC[0:16, 0:32], OZ[:, k * 16:(k + 1) * 16],
                        hpre[:, k * 32:(k + 1) * 32],
                        start=(k == 0), stop=(k == MH - 1)),
                        waits=[p.use("psC")] if k == 0 else ())
                ev = p.do("te", None, inc="te")
                ev_c = p.do("ve", lambda v: v.tensor_scalar(
                    Wscpre[:], psC[0:16, 0:32], scl[0:16, 0:1], None, ALU.mult),
                    waits=[ev], inc="ve")
                p.release("psC", ev_c)
            ev_w2 = load_w2(0, 0)
            evw2 = {0: ev_w2}
            for mo in range(KD):
                par = mo % 2
                if mo + 1 < KD:
                    evw2[mo + 1] = load_w2(mo + 1, (mo + 1) % 2)
                for w in range(FT // 512):
                    bank, bn = (psA, "psA") if w == 0 else (psB, "psB")
                    for k in range(MH):
                        p.do("te", lambda t_, mo=mo, k=k, w=w, par=par, bank=bank, t=t: t_.matmul(
                            bank[:, 0:512], w2m[par][:, k * 128:(k + 1) * 128],
                            h_t[:, k * FT + w * 512: k * FT + (w + 1) * 512],
                            start=(k == 0), stop=False),
                            waits=([evw2[mo], ev_wsc, p.use(bn)] if k == 0 else ()))
                    p.do("te", lambda t_, mo=mo, w=w, bank=bank: t_.matmul(
                        bank[:, 0:512], ET[:, mo * 128:(mo + 1) * 128],
                        Wsc[0:16, w * 512:(w + 1) * 512], start=False, stop=True))
                    ev_mm = p.do("te", None, inc="te")
                    ev_h2 = p.do("sc", lambda s, mo=mo, w=w, bank=bank: s.activation(
                        dtmp[:, w * 512:(w + 1) * 512], bank[:, 0:512],
                        AF.Identity, bias=b2h[:, mo:mo + 1], scale=0.5),
                        waits=[ev_mm, p.use("dt%d" % w + sfx)], inc="ac")
                    ev_e = p.do("ve", lambda v, mo=mo, w=w, t=t: v.tensor_tensor(
                        xf_t[:, mo * FT + w * 512: mo * FT + (w + 1) * 512],
                        dtmp[:, w * 512:(w + 1) * 512],
                        xf_t[:, mo * FT + w * 512: mo * FT + (w + 1) * 512],
                        ALU.add), waits=[ev_h2], inc="ve")
                    p.release("dt%d" % w + sfx, ev_e)
                    p.release(bn, ev_h2)
                ev_st = sdma(dst[mo * 128:(mo + 1) * 128, t * FT:(t + 1) * FT],
                             xf_t[:, mo * FT:(mo + 1) * FT],
                             waits=[p.mark("ve")], stream="st")
                ev_last_store = ev_st
                if with_prep and t == NFT - 1:
                    for k in range(MH):
                        p.do("te", lambda t_, mo=mo, k=k, par=par: t_.matmul(
                            psC[:, 0:32], w2m[par][:, k * 128:(k + 1) * 128],
                            hpre[:, k * 32:(k + 1) * 32],
                            start=(k == 0), stop=False),
                            waits=[p.use("psC")] if k == 0 else ())
                    p.do("te", lambda t_, mo=mo: t_.matmul(
                        psC[:, 0:32], ET[:, mo * 128:(mo + 1) * 128],
                        Wscpre[0:16, :], start=False, stop=True))
                    ev_mm = p.do("te", None, inc="te")
                    ev_h2 = p.do("sc", lambda s, mo=mo: s.activation(
                        ptmp[:], psC[:, 0:32], AF.Identity,
                        bias=b2h[:, mo:mo + 1], scale=0.5),
                        waits=[ev_mm, p.use("pt" + sfx)], inc="ac")
                    ev_e = p.do("ve", lambda v, mo=mo: v.tensor_tensor(
                        x1pre[:, mo * 32:(mo + 1) * 32], ptmp[:],
                        pre_resid[:, mo * 32:(mo + 1) * 32], ALU.add),
                        waits=[ev_h2], inc="ve")
                    p.release("pt" + sfx, ev_e)
                    p.release("psC", ev_h2)
                p.release("w2m%d" % par + sfx, p.mark("te"))
            p.release("xf_t" + sfx, ev_last_store)
        return ev_last_store, p.mark("ve")

    p.do("te", None, waits=[ev_cf, ev_cb, ev_ms, ev_dc])
    ev_xnpre = ln_subtile(lambda k: xpre_f[:, k * 32:(k + 1) * 32],
                          lambda k: xpre_b[:, k * 32:(k + 1) * 32],
                          lambda k: xnpre[:, k * 32:(k + 1) * 32],
                          ev_pf, ev_pb, 32)
    ev_p16 = None
    for k in range(KD):
        ev_p16 = p.do("ve", lambda v, k=k: v.tensor_copy(
            P16a[:, k * 16:(k + 1) * 16], xnpre[:, k * 32 + 1:k * 32 + 17]),
            waits=[ev_xnpre] if k == 0 else (), inc="ve")
    ev_loraA, loraA = prep_lora(ev_p16, b1, P16a, "a")
    ev_ffn1_store, ev_x1pre = ffn_phase(b1, xT, x1sp, ev_loraA, loraA, True,
                                        ev_xnpre, xnpre, xpre_f)
    phase_barrier()
    b1.close()

    attn = contextlib.ExitStack()
    xn2 = mksb(attn, "xn2", [128, KD * TPC], BF16)
    ykall = mksb(attn, "ykall", [128, H * TPC], BF16)
    Scum = mksb(attn, "Scum", [128, H * 7 * 256], BF16)
    vall = mksb(attn, "vall", [128, 16 * 1024], BF16)
    b2aux = contextlib.ExitStack()
    opre = mksb(b2aux, "opre", [128, H * 32], BF16)
    vpre = mksb(b2aux, "vpre", [32, H * 128], BF16)
    x2pre = mksb(b2aux, "x2pre", [128, KD * 16], F32)
    Sstage = mksb(b2aux, "Sstage", [128, 256], F32)

    b2 = contextlib.ExitStack()
    x1f_t = mksb(b2, "x1f_t", [128, KD * 512], F32)
    x1b_t = mksb(b2, "x1b_t", [128, KD * 512], BF16)
    x1v = x1sp.rearrange("(c p) n -> p c n", p=128)
    for w in range(TPC // 512):
        ev_f = sdma(x1f_t[:].rearrange("p (c w) -> p c w", c=KD),
                    x1v[:, :, w * 512:(w + 1) * 512],
                    waits=[ev_ffn1_store, p.use("x1f_t")], stream="x1f")
        ev_b = gdma(x1b_t[:].rearrange("p (c w) -> p c w", c=KD),
                    x1v[:, :, w * 512:(w + 1) * 512],
                    waits=[ev_ffn1_store, p.use("x1b_t")], stream="x1b")
        ev_xn2 = ln_subtile(
            lambda k, w=w: x1f_t[:, k * 512:(k + 1) * 512],
            lambda k, w=w: x1b_t[:, k * 512:(k + 1) * 512],
            lambda k, w=w: xn2[:, k * TPC + w * 512: k * TPC + (w + 1) * 512],
            ev_f, ev_b, 512)
        p.release("x1f_t", p.mark("ve"))
        p.release("x1b_t", p.mark("te"))
    ev_xn2 = p.mark("ve")
    x1pre_b = mksb(b2, "x1pre_b", [128, KD * 32], BF16)
    ev_x1pb = p.do("sc", lambda s: s.activation(x1pre_b[:], x1pre[:], AF.Copy),
                   waits=[ev_x1pre], inc="ac")
    ev_xn2pre = ln_subtile(lambda k: x1pre[:, k * 32:(k + 1) * 32],
                           lambda k: x1pre_b[:, k * 32:(k + 1) * 32],
                           lambda k: xn2pre[:, k * 32:(k + 1) * 32],
                           ev_x1pre, ev_x1pb, 32)
    if DBG:
        sdma(dbg1[:, 0:256], x1pre[:], waits=[ev_x1pre], stream="dbg")
        sdma(dbg2[:, 0:256], xn2pre[:], waits=[ev_xn2pre], stream="dbg")
    wv = mksb(b2, "wv", [128, KD * 1024], BF16)
    ev_wv = gdma(wv[:].rearrange("p (k c) -> p k c", k=KD),
                 WvT.rearrange("(k p) c -> p k c", p=128), stream="wv")
    for tc in range(16):
        for half in range(2):
            bank, bn = (psA, "psA") if half == 0 else (psB, "psB")
            for k in range(KD):
                p.do("te", lambda t, tc=tc, k=k, half=half, bank=bank: t.matmul(
                    bank[:, 0:512], xn2[:, k * TPC + tc * 128:k * TPC + (tc + 1) * 128],
                    wv[:, k * 1024 + half * 512: k * 1024 + (half + 1) * 512],
                    start=(k == 0), stop=(k == KD - 1)),
                    waits=([ev_wv, ev_xn2, p.use(bn)] if k == 0 else ()))
            ev = p.do("te", None, inc="te")
            ev_c = p.do("ve", lambda v, tc=tc, half=half, bank=bank: v.tensor_copy(
                vall[:, tc * 1024 + half * 512: tc * 1024 + (half + 1) * 512],
                bank[:, 0:512]), waits=[ev], inc="ve")
            p.release(bn, ev_c)
    ev_vall = p.mark("ve")
    for h in range(H):
        for k in range(KD):
            p.do("te", lambda t, h=h, k=k: t.matmul(
                psC[0:32, 0:128], xn2pre[:, k * 32:(k + 1) * 32],
                wv[:, k * 1024 + h * 128: k * 1024 + (h + 1) * 128],
                start=(k == 0), stop=(k == KD - 1)),
                waits=([ev_wv, ev_xn2pre, p.use("psC")] if k == 0 else ()))
        ev = p.do("te", None, inc="te")
        ev_c = p.do("ve", lambda v, h=h: v.tensor_copy(
            vpre[:, h * 128:(h + 1) * 128], psC[0:32, 0:128]),
            waits=[ev], inc="ve")
        p.release("psC", ev_c)
    ev_vpre = p.mark("ve")
    phase_barrier()
    b2.close()

    b2h_st = contextlib.ExitStack()
    wkh = [mksb(b2h_st, "wkh0", [128, KD * 128], BF16),
           mksb(b2h_st, "wkh1", [128, KD * 128], BF16)]
    wqh = [mksb(b2h_st, "wqh0", [128, KD * 128], BF16),
           mksb(b2h_st, "wqh1", [128, KD * 128], BF16)]
    kT = mksb(b2h_st, "kT", [128, TPC], BF16)
    ktok = mksb(b2h_st, "ktok", [128, 16 * 256], BF16)
    qpreT = mksb(b2h_st, "qpreT", [128, 32], BF16)
    kpreT = mksb(b2h_st, "kpreT", [128, 32], BF16)
    qpf = mksb(b2h_st, "qpf", [128, 2 * 32], BF16)
    kpf = mksb(b2h_st, "kpf", [128, 2 * 32], BF16)
    Apre = mksb(b2h_st, "Apre", [32, 32], BF16)
    Scur = mksb(b2h_st, "Scur", [128, 256], F32)
    osqp = mksb(b2h_st, "osqp", [128, 32], BF16)
    rbp = mksb(b2h_st, "rbp", [128, 32], F32)

    def head_w(dst, par, name, W, h):
        return gdma(dst[par][:].rearrange("p (k c) -> p k c", k=KD),
                    W[:, h * 128:(h + 1) * 128].rearrange("(k p) c -> p k c", p=128),
                    waits=[p.use(name + str(par))], stream=name + str(par))

    evk = {0: head_w(wkh, 0, "wkh", WkT, 0)}
    evq = {0: head_w(wqh, 0, "wqh", WqT, 0)}
    for h in range(H):
        par = h % 2
        if h + 1 < H:
            evk[h + 1] = head_w(wkh, (h + 1) % 2, "wkh", WkT, h + 1)
            evq[h + 1] = head_w(wqh, (h + 1) % 2, "wqh", WqT, h + 1)
        for w in range(4):
            bank, bn = (psA, "psA") if w % 2 == 0 else (psB, "psB")
            for k in range(KD):
                p.do("te", lambda t, h=h, k=k, w=w, par=par, bank=bank: t.matmul(
                    bank[:, 0:512], wkh[par][:, k * 128:(k + 1) * 128],
                    xn2[:, k * TPC + w * 512: k * TPC + (w + 1) * 512],
                    start=(k == 0), stop=(k == KD - 1)),
                    waits=([evk[h], ev_xn2, p.use(bn)] if k == 0 else ()))
            ev = p.do("te", None, inc="te")
            ev_c = p.do("ve", lambda v, w=w, bank=bank: v.tensor_copy(
                kT[:, w * 512:(w + 1) * 512], bank[:, 0:512]),
                waits=[ev, p.use("kT") if w == 0 else None], inc="ve")
            p.release(bn, ev_c)
        ev_kT = p.mark("ve")
        for w in range(4):
            bank, bn = (psC, "psC") if w % 2 == 0 else (psD, "psD")
            ev = p.do("te", lambda t, w=w, bank=bank: t.matmul(
                bank[:, 0:512], fmkT_s[:], kT[:, w * 512:(w + 1) * 512],
                start=True, stop=True),
                waits=[ev_kT, p.use(bn)], inc="te")
            ev_c = p.do("ve", lambda v, h=h, w=w, bank=bank: v.tensor_copy(
                ykall[:, h * TPC + w * 512: h * TPC + (w + 1) * 512],
                bank[:, 0:512]), waits=[ev], inc="ve")
            p.release(bn, ev_c)
        for g in range(4):
            bank, bn = (psE, "psE") if g % 2 == 0 else (psF, "psF")
            for j in range(4):
                tc = g * 4 + j
                p.do("te", lambda t, tc=tc, j=j, bank=bank: t.matmul(
                    bank[:, j * 128:(j + 1) * 128],
                    kT[:, tc * 128:(tc + 1) * 128], fmkT_s[:],
                    start=True, stop=False),
                    waits=([ev_kT, p.use(bn)] if j == 0 else ()))
                p.do("te", lambda t, tc=tc, j=j, bank=bank: t.matmul(
                    bank[:, j * 128:(j + 1) * 128],
                    onesr_b[:], fmkbr_s[:], start=False, stop=True))
            ev = p.do("te", None, inc="te")
            p.do("sc", lambda s, g=g, bank=bank: s.activation(
                ktok[:, g * 512:(g + 1) * 512], bank[:, 0:512], AF.Exp),
                waits=[ev, p.use("ktok") if g == 0 else None])
            ev_x = p.do("sc", lambda s, g=g, bank=bank: s.activation(
                ktok[:, 2048 + g * 512: 2048 + (g + 1) * 512],
                bank[:, 0:512], AF.Exp, scale=-1.0), inc="ac")
            p.release(bn, ev_x)
        ev_ktok = p.mark("ac")
        p.release("kT", p.mark("te"))
        for sc_i in range(NSC):
            for tc2 in range(2):
                tc = sc_i * 2 + tc2
                for fc in range(2):
                    bank = psA if fc == 0 else psB
                    p.do("te", lambda t, tc=tc, fc=fc, h=h, bank=bank,
                         tc2=tc2: t.matmul(
                        bank[:, 0:128],
                        ktok[:, fc * 2048 + tc * 128: fc * 2048 + (tc + 1) * 128],
                        vall[:, tc * 1024 + h * 128: tc * 1024 + (h + 1) * 128],
                        start=(tc2 == 0), stop=(tc2 == 1)),
                        waits=([ev_ktok, ev_vall, p.use("psA"), p.use("psB")]
                               if (tc2 == 0 and fc == 0) else ()))
            ev_s = p.do("te", None, inc="te")
            ev_add = None
            for fc in range(2):
                bank = psA if fc == 0 else psB
                if sc_i == 0:
                    ev_add = p.do("ve", lambda v, fc=fc, bank=bank: v.tensor_copy(
                        Scur[:, fc * 128:(fc + 1) * 128], bank[:, 0:128]),
                        waits=([ev_s, p.use("Scur")] if fc == 0 else ()),
                        inc="ve")
                else:
                    ev_add = p.do("ve", lambda v, fc=fc, bank=bank: v.tensor_tensor(
                        Scur[:, fc * 128:(fc + 1) * 128], bank[:, 0:128],
                        Scur[:, fc * 128:(fc + 1) * 128], ALU.add),
                        waits=([ev_s, p.use("Scur")] if fc == 0 else ()),
                        inc="ve")
            p.release("psA", ev_add)
            p.release("psB", ev_add)
            p.release("Scur", ev_add)
            if sc_i < NSC - 1:
                ev_cp = p.do("sc", lambda s, h=h, sc_i=sc_i: s.copy(
                    Scum[:, (h * 7 + sc_i) * 256:(h * 7 + sc_i + 1) * 256],
                    Scur[:]), waits=[ev_add], inc="ac")
                p.release("Scur", ev_cp)
            else:
                ev_st = p.do("sc", lambda s: s.copy(Sstage[:], Scur[:]),
                             waits=[ev_add, p.use("Sstage")], inc="ac")
                p.release("Scur", ev_st)
                ev_dma = sdma(ccin.ap()[h * 256:(h + 1) * 256, :]
                              .rearrange("(fc p) d -> p fc d", p=128),
                              Sstage[:].rearrange("p (fc d) -> p fc d", fc=2),
                              waits=[ev_st], stream="ci")
                p.release("Sstage", ev_dma)
        p.release("ktok", p.mark("te"))
        for dstT, wt, evw, nm in ((qpreT, wqh, evq, "q"), (kpreT, wkh, evk, "k")):
            for k in range(KD):
                p.do("te", lambda t, dstT=dstT, wt=wt, k=k, par=par, h=h, evw=evw: t.matmul(
                    psC[:, 0:32], wt[par][:, k * 128:(k + 1) * 128],
                    xn2pre[:, k * 32:(k + 1) * 32],
                    start=(k == 0), stop=(k == KD - 1)),
                    waits=([evw[h], ev_xn2pre, p.use("psC")] if k == 0 else ()))
            ev = p.do("te", None, inc="te")
            ev_c = p.do("ve", lambda v, dstT=dstT: v.tensor_copy(dstT[:], psC[:, 0:32]),
                        waits=[ev], inc="ve")
            p.release("psC", ev_c)
        ev_pre_kq = p.mark("ve")
        p.release("wqh%d" % par, p.mark("te"))
        p.release("wkh%d" % par, p.mark("te"))
        for srcT, dstF, fmw, bpos, bneg in (
                (qpreT, qpf, fmqT_s, fmqb_m, fmqb_mn),
                (kpreT, kpf, fmkT_s, fmkb_s, fmkb_n)):
            ev = p.do("te", lambda t, srcT=srcT, fmw=fmw: t.matmul(
                psC[:, 0:32], fmw[:], srcT[:], start=True, stop=True),
                waits=[ev_pre_kq, p.use("psC")], inc="te")
            p.do("sc", lambda s, dstF=dstF, bpos=bpos: s.activation(
                dstF[:, 0:32], psC[:, 0:32], AF.Exp, bias=bpos[:]), waits=[ev])
            ev_x = p.do("sc", lambda s, dstF=dstF, bneg=bneg: s.activation(
                dstF[:, 32:64], psC[:, 0:32], AF.Exp, bias=bneg[:], scale=-1.0),
                inc="ac")
            p.release("psC", ev_x)
        ev_pf_ = p.mark("ac")
        for fc in range(2):
            p.do("te", lambda t, fc=fc: t.matmul(
                psC[0:32, 0:32], kpf[:, fc * 32:(fc + 1) * 32],
                qpf[:, fc * 32:(fc + 1) * 32],
                start=(fc == 0), stop=(fc == 1)),
                waits=([ev_pf_, p.use("psC")] if fc == 0 else ()))
        ev = p.do("te", None, inc="te")
        ev_am = p.do("ve", lambda v: v.tensor_tensor(
            Apre[:], psC[0:32, 0:32], maskA[0:32, 0:32], ALU.mult),
            waits=[ev, p.use("Apre")], inc="ve")
        p.release("psC", ev_am)
        ev_o = p.do("te", lambda t, h=h: t.matmul(
            psD[:, 0:32], vpre[:, h * 128:(h + 1) * 128], Apre[:],
            start=True, stop=True), waits=[ev_am, ev_vpre, p.use("psD")], inc="te")
        ev_sq = p.do("sc", lambda s: s.activation(osqp[:], psD[:, 0:32], AF.Square),
                     waits=[ev_o, p.use("osqp")], inc="ac")
        ev_ss = p.do("te", lambda t: t.matmul(
            psR1[0:1, 0:32], onesb[:], osqp[:], start=True, stop=True),
            waits=[ev_sq, p.use("psR1")], inc="te")
        p.release("osqp", ev_ss)
        ev_t = p.do("ve", lambda v: v.tensor_scalar(
            mrow[0:1, 0:32], psR1[0:1, 0:32], 1.0 / 128.0, 1e-5,
            ALU.mult, ALU.add), waits=[ev_ss, p.use("mrow")], inc="ve")
        p.release("psR1", ev_t)
        ev_rc = p.do("ve", lambda v: v.reciprocal(
            mrow[0:1, 0:32], mrow[0:1, 0:32]), waits=[ev_t], inc="ve")
        ev_r = p.do("sc", lambda s: s.activation(
            rrow[0:1, 0:32], mrow[0:1, 0:32], AF.Sqrt),
            waits=[ev_rc, p.use("rrow")], inc="ac")
        p.release("mrow", ev_r)
        ev_bc = p.do("te", lambda t: t.matmul(
            psE[:, 0:32], onesr_f[:], rrow[0:1, 0:32], start=True, stop=True),
            waits=[ev_r, p.use("psE")], inc="te")
        p.release("rrow", ev_bc)
        ev_rb = p.do("ve", lambda v: v.tensor_copy(rbp[:], psE[:, 0:32]),
                     waits=[ev_bc, p.use("rbp")], inc="ve")
        p.release("psE", ev_rb)
        ev_op = p.do("ve", lambda v, h=h: v.scalar_tensor_tensor(
            opre[:, h * 32:(h + 1) * 32], psD[:, 0:32], rmsw_s[:],
            rbp[:], ALU.mult, ALU.mult), waits=[ev_rb], inc="ve")
        p.release("psD", ev_op)
        p.release("rbp", ev_op)
    ev_opre = p.mark("ve")
    phase_barrier()
    b2h_st.close()

    b2w = contextlib.ExitStack()
    woh = [mksb(b2w, "woh0", [128, KD * 128], BF16),
           mksb(b2w, "woh1", [128, KD * 128], BF16)]
    evo = {0: gdma(woh[0][:].rearrange("p (k c) -> p k c", k=KD),
                   WoT[:, 0:128].rearrange("(k p) c -> p k c", p=128),
                   stream="woh0")}
    ev_ccx = None
    for mo in range(KD):
        par = mo % 2
        if mo + 1 < KD:
            evo[mo + 1] = gdma(woh[(mo + 1) % 2][:].rearrange("p (k c) -> p k c", k=KD),
                               WoT[:, (mo + 1) * 128:(mo + 2) * 128]
                               .rearrange("(k p) c -> p k c", p=128),
                               waits=[p.use("woh%d" % ((mo + 1) % 2))],
                               stream="woh%d" % ((mo + 1) % 2))
        for hh in range(H):
            p.do("te", lambda t, mo=mo, hh=hh, par=par: t.matmul(
                psC[:, 0:16], woh[par][:, hh * 128:(hh + 1) * 128],
                opre[:, hh * 32 + 1: hh * 32 + 17],
                start=(hh == 0), stop=(hh == H - 1)),
                waits=([evo[mo], ev_opre, p.use("psC")] if hh == 0 else ()))
        ev = p.do("te", None, inc="te")
        ev_x2p = p.do("ve", lambda v, mo=mo: v.tensor_tensor(
            x2pre[:, mo * 16:(mo + 1) * 16], psC[:, 0:16],
            x1pre[:, mo * 32 + 1: mo * 32 + 17], ALU.add),
            waits=[ev], inc="ve")
        p.release("psC", ev_x2p)
        p.release("woh%d" % par, p.mark("te"))
    ev_x2pre = p.mark("ve")
    ev_ccx = sdma(ccin.ap()[2048:2176, :], x2pre[:], waits=[ev_x2pre],
                  stream="ci")
    phase_barrier()
    b2w.close()
    b2aux.close()
    ev_cc = p.do("gp", lambda g: g.collective_compute(
        "AllGather", ALU.bypass, replica_groups=[list(range(NCORE))],
        ins=[ccin.ap().opt()], outs=[ccout.ap().opt()]),
        waits=[p.mark("Dci")], inc="cc")

    oall = mksb(attn, "oall", [128, H * TPC], BF16)
    Sprefb = mksb(attn, "Sprefb", [128, H * 256], BF16)
    b3p = contextlib.ExitStack()
    acc = mksb(b3p, "acc", [128, 16 * 128], F32)
    stg = [mksb(b3p, "stg0", [128, 16 * 128], F32),
           mksb(b3p, "stg1", [128, 16 * 128], F32)]
    ccov = ccout.ap()
    ev_acc = None
    for i in range(NCORE):
        ev_ld = sdma(stg[i % 2][:].rearrange("p (g d) -> p g d", g=16),
                     ccov[i * 2176:i * 2176 + 2048, :]
                     .rearrange("(g p) d -> p g d", p=128),
                     waits=[ev_cc, p.use("stg%d" % (i % 2))],
                     stream="stg%d" % (i % 2))
        if i == 0:
            ev_acc = p.do("ve", lambda v, i=i: v.tensor_scalar(
                acc[:], stg[0][:], prefm[:, 0:1], None, ALU.mult),
                waits=[ev_ld], inc="ve")
        else:
            ev_acc = p.do("ve", lambda v, i=i: v.scalar_tensor_tensor(
                acc[:], stg[i % 2][:], prefm[:, i:i + 1], acc[:],
                ALU.mult, ALU.add), waits=[ev_ld, ev_acc], inc="ve")
        p.release("stg%d" % (i % 2), ev_acc)
    ev_spf = p.do("ve", lambda v: v.tensor_copy(Sprefb[:], acc[:]),
                  waits=[ev_acc], inc="ve")
    phase_barrier()
    b3p.close()

    b3h = contextlib.ExitStack()
    _wq2 = mksb(b3h, "wq20", [128, KD * 128], BF16)
    wqh2 = [_wq2, _wq2]
    qT2 = mksb(b3h, "qT2", [128, TPC], BF16)
    qf = mksb(b3h, "qf", [128, 2 * TPC], BF16)
    kf = mksb(b3h, "kf", [128, 2 * TPC], BF16)
    Am = mksb(b3h, "Am", [128, 2 * 256], BF16)
    osq = mksb(b3h, "osq", [128, 256], BF16)
    rst = mksb(b3h, "rst", [1, TPC], F32)
    evq2 = {0: gdma(wqh2[0][:].rearrange("p (k c) -> p k c", k=KD),
                    WqT[:, 0:128].rearrange("(k p) c -> p k c", p=128),
                    stream="wq20")}
    pend_norm = [None]

    def flush_norm():
        if pend_norm[0] is None:
            return
        hh, ev_sqrt_ = pend_norm[0]
        pend_norm[0] = None
        for w in range(4):
            ev_b = p.do("te", lambda t, w=w: t.matmul(
                psC[:, 0:512], onesr_f[:], rst[0:1, w * 512:(w + 1) * 512],
                start=True, stop=True),
                waits=[ev_sqrt_ if w == 0 else None, p.use("psC")], inc="te")
            ev_ap = p.do("ve", lambda v, hh=hh, w=w: v.scalar_tensor_tensor(
                oall[:, hh * TPC + w * 512: hh * TPC + (w + 1) * 512],
                oall[:, hh * TPC + w * 512: hh * TPC + (w + 1) * 512],
                rmsw_s[:], psC[:, 0:512], ALU.mult, ALU.mult),
                waits=[ev_b], inc="ve")
            p.release("psC", ev_ap)
        p.release("rst", p.mark("te"))

    for h in range(H):
        par = h % 2
        for w in range(4):
            bank, bn = (psA, "psA") if w % 2 == 0 else (psB, "psB")
            for k in range(KD):
                p.do("te", lambda t, h=h, k=k, w=w, par=par, bank=bank: t.matmul(
                    bank[:, 0:512], wqh2[par][:, k * 128:(k + 1) * 128],
                    xn2[:, k * TPC + w * 512: k * TPC + (w + 1) * 512],
                    start=(k == 0), stop=(k == KD - 1)),
                    waits=([evq2[h], p.use(bn)] if k == 0 else ()))
            ev = p.do("te", None, inc="te")
            ev_c = p.do("ve", lambda v, w=w, bank=bank: v.tensor_copy(
                qT2[:, w * 512:(w + 1) * 512], bank[:, 0:512]),
                waits=[ev, p.use("qT2") if w == 0 else None], inc="ve")
            p.release(bn, ev_c)
        ev_qT = p.mark("ve")
        p.release("wq20", p.mark("te"))
        if h + 1 < H:
            evq2[h + 1] = gdma(wqh2[(h + 1) % 2][:].rearrange("p (k c) -> p k c", k=KD),
                               WqT[:, (h + 1) * 128:(h + 2) * 128]
                               .rearrange("(k p) c -> p k c", p=128),
                               waits=[p.use("wq20")],
                               stream="wq20")
        for w in range(4):
            bank, bn = (psC, "psC") if w % 2 == 0 else (psD, "psD")
            ev = p.do("te", lambda t, w=w, bank=bank: t.matmul(
                bank[:, 0:512], fmqT_s[:], qT2[:, w * 512:(w + 1) * 512],
                start=True, stop=True), waits=[ev_qT, p.use(bn)], inc="te")
            p.do("sc", lambda s, w=w, bank=bank: s.activation(
                qf[:, w * 512:(w + 1) * 512], bank[:, 0:512],
                AF.Exp, bias=fmqb_m[:]),
                waits=[ev, p.use("qf") if w == 0 else None])
            ev_x = p.do("sc", lambda s, w=w, bank=bank: s.activation(
                qf[:, TPC + w * 512: TPC + (w + 1) * 512], bank[:, 0:512],
                AF.Exp, bias=fmqb_mn[:], scale=-1.0), inc="ac")
            p.release(bn, ev_x)
        p.release("qT2", p.mark("te"))
        flush_norm()
        for w in range(4):
            p.do("sc", lambda s, h=h, w=w: s.activation(
                kf[:, w * 512:(w + 1) * 512],
                ykall[:, h * TPC + w * 512: h * TPC + (w + 1) * 512],
                AF.Exp, bias=fmkb_s[:]),
                waits=[p.use("kf")] if w == 0 else ())
            p.do("sc", lambda s, h=h, w=w: s.activation(
                kf[:, TPC + w * 512: TPC + (w + 1) * 512],
                ykall[:, h * TPC + w * 512: h * TPC + (w + 1) * 512],
                AF.Exp, bias=fmkb_n[:], scale=-1.0))
        ev_kf = p.do("sc", None, inc="ac")
        for sci in range(NSC):
            obank, obn = (psE, "psE") if sci % 2 == 0 else (psF, "psF")
            for ktc in range(2):
                bank, bn = (psA, "psA") if ktc == 0 else (psB, "psB")
                for fc in range(2):
                    p.do("te", lambda t, sci=sci, ktc=ktc, fc=fc: t.matmul(
                        (psA if ktc == 0 else psB)[:, 0:256],
                        kf[:, fc * TPC + (sci * 2 + ktc) * 128:
                           fc * TPC + (sci * 2 + ktc) * 128 + 128],
                        qf[:, fc * TPC + sci * 256: fc * TPC + sci * 256 + 256],
                        start=(fc == 0), stop=(fc == 1)),
                        waits=([ev_kf, p.mark("ac"), p.use(bn)] if fc == 0 else ()))
                ev = p.do("te", None, inc="te")
                ev_c = p.do("ve", lambda v, ktc=ktc, bank=bank: v.tensor_tensor(
                    Am[:, ktc * 256:(ktc + 1) * 256], bank[:, 0:256],
                    maskA[:, ktc * 256:(ktc + 1) * 256], ALU.mult),
                    waits=[ev, p.use("Am") if ktc == 0 else None], inc="ve")
            ev_am = ev_c
            p.release("psA", ev_am)
            p.release("psB", ev_am)
            first = True
            for ktc in range(2):
                tc = sci * 2 + ktc
                p.do("te", lambda t, h=h, tc=tc, ktc=ktc, obank=obank, first=first: t.matmul(
                    obank[:, 0:256],
                    vall[:, tc * 1024 + h * 128: tc * 1024 + (h + 1) * 128],
                    Am[:, ktc * 256:(ktc + 1) * 256],
                    start=first, stop=False),
                    waits=([ev_am, p.use(obn)] if first else [ev_am]))
                first = False
            if sci > 0:
                for fc in range(2):
                    p.do("te", lambda t, h=h, sci=sci, fc=fc, obank=obank: t.matmul(
                        obank[:, 0:256],
                        Scum[:, (h * 7 + sci - 1) * 256 + fc * 128:
                             (h * 7 + sci - 1) * 256 + (fc + 1) * 128],
                        qf[:, fc * TPC + sci * 256: fc * TPC + sci * 256 + 256],
                        start=False, stop=False))
            for fc in range(2):
                p.do("te", lambda t, h=h, sci=sci, fc=fc, obank=obank: t.matmul(
                    obank[:, 0:256],
                    Sprefb[:, h * 256 + fc * 128: h * 256 + (fc + 1) * 128],
                    qf[:, fc * TPC + sci * 256: fc * TPC + sci * 256 + 256],
                    start=False, stop=(fc == 1)),
                    waits=[ev_spf] if (fc == 0 and sci == 0) else ())
            ev_o = p.do("te", None, inc="te")
            ev_cp2 = p.do("ve", lambda v, h=h, sci=sci, obank=obank: v.tensor_copy(
                oall[:, h * TPC + sci * 256: h * TPC + (sci + 1) * 256],
                obank[:, 0:256]), waits=[ev_o], inc="ve")
            ev_sq2 = p.do("sc", lambda s, obank=obank: s.activation(
                osq[:], obank[:, 0:256], AF.Square),
                waits=[ev_o, ev_cp2, p.use("osq")], inc="ac")
            ev_ss2 = p.do("te", lambda t: t.matmul(
                psR1[0:1, 0:256], onesb[:], osq[:], start=True, stop=True),
                waits=[ev_sq2, p.use("psR1")], inc="te")
            p.release("osq", ev_ss2)
            p.release(obn, ev_ss2)
            ev_stw = p.do("ve", lambda v, sci=sci: v.tensor_scalar(
                rst[0:1, sci * 256:(sci + 1) * 256], psR1[0:1, 0:256],
                1.0 / 128.0, 1e-5, ALU.mult, ALU.add),
                waits=([ev_ss2, p.use("rst")] if sci == 0
                       else [ev_ss2]), inc="ve")
            p.release("psR1", ev_stw)
            p.release("Am", p.mark("te"))
        ev_rc2 = p.do("ve", lambda v: v.reciprocal(
            rst[0:1, :], rst[0:1, :]), waits=[ev_stw], inc="ve")
        ev_sqrt2 = p.do("sc", lambda s: s.activation(
            rst[0:1, :], rst[0:1, :], AF.Sqrt),
            waits=[ev_rc2], inc="ac")
        pend_norm[0] = (h, ev_sqrt2)
        p.release("qf", p.mark("te"))
        p.release("kf", p.mark("te"))
    flush_norm()
    ev_oall = p.mark("ve")
    phase_barrier()
    b3h.close()

    b3w = contextlib.ExitStack()
    woh2 = [mksb(b3w, "wo20", [128, KD * 128], BF16),
            mksb(b3w, "wo21", [128, KD * 128], BF16)]
    x1r = mksb(b3w, "x1r", [128, TPC], F32)
    evo2 = {0: gdma(woh2[0][:].rearrange("p (k c) -> p k c", k=KD),
                    WoT[:, 0:128].rearrange("(k p) c -> p k c", p=128),
                    stream="wo20")}
    ev_x2st = None
    for mo in range(KD):
        par = mo % 2
        if mo + 1 < KD:
            evo2[mo + 1] = gdma(woh2[(mo + 1) % 2][:].rearrange("p (k c) -> p k c", k=KD),
                                WoT[:, (mo + 1) * 128:(mo + 2) * 128]
                                .rearrange("(k p) c -> p k c", p=128),
                                waits=[p.use("wo2%d" % ((mo + 1) % 2))],
                                stream="wo2%d" % ((mo + 1) % 2))
        ev_x1r = sdma(x1r[:], x1sp[mo * 128:(mo + 1) * 128, :],
                      waits=[p.use("x1r")], stream="x1r")
        for w in range(4):
            bank, bn = (psA, "psA") if w % 2 == 0 else (psB, "psB")
            for hh in range(H):
                p.do("te", lambda t, mo=mo, hh=hh, w=w, par=par, bank=bank: t.matmul(
                    bank[:, 0:512], woh2[par][:, hh * 128:(hh + 1) * 128],
                    oall[:, hh * TPC + w * 512: hh * TPC + (w + 1) * 512],
                    start=(hh == 0), stop=(hh == H - 1)),
                    waits=([evo2[mo], ev_oall, p.use(bn)] if hh == 0 else ()))
            ev = p.do("te", None, inc="te")
            ev_e = p.do("ve", lambda v, w=w, bank=bank: v.tensor_tensor(
                x1r[:, w * 512:(w + 1) * 512], bank[:, 0:512],
                x1r[:, w * 512:(w + 1) * 512], ALU.add),
                waits=[ev, ev_x1r], inc="ve")
            p.release(bn, ev_e)
        ev_x2st = sdma(x2sp[mo * 128:(mo + 1) * 128, :], x1r[:],
                       waits=[p.mark("ve")], stream="st")
        p.release("x1r", ev_x2st)
        p.release("wo2%d" % par, p.mark("te"))
    phase_barrier()
    b3w.close()
    attn.close()

    b4 = contextlib.ExitStack()
    xp2f = mksb(b4, "xp2f", [128, KD * 16], F32)
    xp2b = mksb(b4, "xp2b", [128, KD * 16], BF16)
    ev_g1 = sdma(xp2f[:], ccov[2048:2176, :], waits=[ev_cc], stream="xp")
    ev_g2 = gdma(xp2b[:], ccov[2048:2176, :], waits=[ev_cc], stream="xpb")
    P16b = mksb(b4, "P16b", [128, KD * 16], BF16)
    ev_p16b = ln_subtile(lambda k: xp2f[:, k * 16:(k + 1) * 16],
                         lambda k: xp2b[:, k * 16:(k + 1) * 16],
                         lambda k: P16b[:, k * 16:(k + 1) * 16],
                         ev_g1, ev_g2, 16)
    ev_loraB, loraB = prep_lora(ev_p16b, b4, P16b, "b")
    ev_out_store, _ = ffn_phase(b4, x2sp, outT, ev_loraB, loraB, False,
                                None, None, None, sfx="b")
    b4.close()

    ws = [p.mark(s) for s in list(p.semv) if s.startswith("D")]
    p.do("sy", None, waits=ws)
    p.do("gp", None, waits=ws)

    with nc.Block() as blk:
        blk.tensor(lambda t: p.replay("te", t))
        blk.vector(lambda v: p.replay("ve", v))
        blk.scalar(lambda s: p.replay("sc", s))
        blk.gpsimd(lambda g: p.replay("gp", g))
        blk.sync(lambda s: p.replay("sy", s))
    psbank.close()
    top.close()
    return nc


_NC_CACHE = [None]


def kernel(L0_attn_x, L0_attn_Wq, L0_attn_Wk, L0_attn_Wv, L0_attn_Wo,
           L0_attn_fmq_w, L0_attn_fmq_b, L0_attn_fmk_w, L0_attn_fmk_b,
           L0_attn_rms_w, L0_ffn_W1, L0_ffn_b1, L0_ffn_W2, L0_ffn_b2,
           L0_ffn_scale):
    f32 = np.float32
    xTf = np.ascontiguousarray(np.asarray(L0_attn_x, dtype=f32)[0].T)
    com = dict(
        xTpre=np.ascontiguousarray(xTf[:, 0:32]),
        WqT=np.ascontiguousarray(np.asarray(L0_attn_Wq, f32).T),
        WkT=np.ascontiguousarray(np.asarray(L0_attn_Wk, f32).T),
        WvT=np.ascontiguousarray(np.asarray(L0_attn_Wv, f32).T),
        WoT=np.ascontiguousarray(np.asarray(L0_attn_Wo, f32).T),
        W1T=np.ascontiguousarray(np.asarray(L0_ffn_W1, f32).T),
        W2T=np.ascontiguousarray(np.asarray(L0_ffn_W2, f32).T),
        fmqT=np.ascontiguousarray(np.asarray(L0_attn_fmq_w, f32).T),
        fmkT=np.ascontiguousarray(np.asarray(L0_attn_fmk_w, f32).T),
        fmqb_dr=np.asarray(L0_attn_fmq_b, f32).reshape(128, 1).copy(),
        fmkb_dr=np.asarray(L0_attn_fmk_b, f32).reshape(128, 1).copy(),
        fmkb_r=np.asarray(L0_attn_fmk_b, f32).reshape(1, 128).copy(),
        rmsw_dr=np.asarray(L0_attn_rms_w, f32).reshape(128, 1).copy(),
        b1c_dr=np.ascontiguousarray(np.asarray(L0_ffn_b1, f32).reshape(MH, 128).T),
        b2c_dr=np.ascontiguousarray(np.asarray(L0_ffn_b2, f32).reshape(KD, 128).T),
        scl_dr=np.full((128, 1), float(np.asarray(L0_ffn_scale)), f32),
        maskA_dr=np.concatenate([np.triu(np.ones((128, 256), f32)),
                              np.triu(np.ones((128, 256), f32), 128)], axis=1),
        maskup_dr=np.eye(4, dtype=f32),
        eye16r_dr=np.eye(16, dtype=f32).reshape(1, 256).copy(),
        eye16m_dr=np.eye(16, dtype=f32),
        identd=np.eye(128, dtype=f32),
        cpoold=np.tile(np.array([1e-5, -0.5, 0.5, 1.0 / 1024.0, 1.0 / 128.0,
                                 -1.0, 2.0, 1.0], f32), (128, 1)),
    )
    in_maps = []
    for c in range(NCORE):
        pm = np.zeros((128, NCORE), f32)
        pm[:, :c] = 1.0
        m = dict(com)
        m["xT"] = np.ascontiguousarray(xTf[:, c * TPC:(c + 1) * TPC])
        m["prefm_dr"] = pm
        in_maps.append(m)

    if _NC_CACHE[0] is None:
        _NC_CACHE[0] = build_program()
    nc = _NC_CACHE[0]
    res = run_bass_kernel_spmd(nc, in_maps, core_ids=list(range(NCORE)))
    parts = [np.asarray(r["outT"]) for r in res.results]
    out = np.concatenate(parts, axis=1).T.copy()
    return out.reshape(1, N, D).astype(np.float32)
